# version 01
# baseline (speedup 1.0000x reference)
import numpy as np
import ml_dtypes
import concourse.bass as bass
import concourse.mybir as mybir
from concourse.bass import IndirectOffsetOnAxis
from concourse.bass_utils import run_bass_kernel_spmd

F32 = mybir.dt.float32
BF16 = mybir.dt.bfloat16
U32 = mybir.dt.uint32
AF = mybir.ActivationFunctionType
ALU = mybir.AluOpType
AX = mybir.AxisListType
BF16NP = np.dtype(ml_dtypes.bfloat16)

NCORES = 8
S = 8192
D = 4096
TPC = 1024
NCH = 32
BCONST = 30.0
SC_ATT = 1.0 / float(np.sqrt(128.0))
SC_R = 1.0 / float(np.sqrt(48.0))
GC = 0.7978845608028654
GCA = GC * 0.044715

_STAGE = 3

ENGINES = ("tensor", "vector", "scalar", "gpsimd", "sync")


class Builder:
    def __init__(self):
        self.nc = bass.Bass("TRN2", target_bir_lowering=False)
        self.stream = {e: [] for e in ENGINES}
        self.sems = {}
        self.cnt = {}
        self.items = []
        for s in ("SY", "SC", "GP", "PE", "DV", "AC", "CC"):
            self.sems[s] = self.nc.alloc_semaphore(s)
            self.cnt[s] = 0
        self.SBUF_BASE = 16512
        self.SBUF_MAX = 229344

    def emit(self, eng, fn, inc=None, n=1):
        rec = {"eng": eng, "pos": len(self.stream[eng]), "waits": [], "incs": []}
        if inc is not None:
            self.cnt[inc] += n
            rec["incs"].append((inc, self.cnt[inc]))
            h = self.sems[inc]

            def wrapped(e, fn=fn, h=h, n=n):
                fn(e).then_inc(h, n)

            self.stream[eng].append(wrapped)
        else:
            self.stream[eng].append(fn)
        self.items.append(rec)
        return self.cnt[inc] if inc else None

    def wait(self, eng, sem, val):
        if val <= 0:
            return
        h = self.sems[sem]
        rec = {"eng": eng, "pos": len(self.stream[eng]), "waits": [(sem, val)], "incs": []}
        self.stream[eng].append(lambda e, h=h, v=val: e.wait_ge(h, v))
        self.items.append(rec)

    def snap(self, sem):
        return self.cnt[sem]

    def at(self, name, shape, dtype, off):
        esz = 4 if dtype in (F32, U32) else 2
        nbytes = int(np.prod(shape[1:])) * esz
        assert off >= self.SBUF_BASE and off + nbytes <= self.SBUF_MAX, \
            f"{name}: [{off}, {off}+{nbytes}) outside [{self.SBUF_BASE}, {self.SBUF_MAX})"
        return self.nc.alloc_sbuf_tensor_at(name, list(shape), dtype, offset=off).ap()

    def check(self):
        final = dict(self.cnt)
        producers = {}
        for i, rec in enumerate(self.items):
            for sem, cum in rec["incs"]:
                producers.setdefault(sem, []).append((cum, i))
        n = len(self.items)
        adj = [[] for _ in range(n)]
        laste = {}
        for i, rec in enumerate(self.items):
            e = rec["eng"]
            if e in laste:
                adj[laste[e]].append(i)
            laste[e] = i
            for sem, v in rec["waits"]:
                assert v <= final[sem], f"wait {sem}>={v} but final={final[sem]}"
                src = None
                for cum, j in producers.get(sem, []):
                    if cum >= v:
                        src = j
                        break
                assert src is not None, (sem, v)
                adj[src].append(i)
        state = [0] * n
        order_pos = {}
        stack = []
        for s0 in range(n):
            if state[s0]:
                continue
            stack.append((s0, iter(adj[s0])))
            state[s0] = 1
            while stack:
                node, it = stack[-1]
                advanced = False
                for nxt in it:
                    if state[nxt] == 1:
                        raise RuntimeError(
                            f"sync cycle: item {nxt} eng={self.items[nxt]['eng']} "
                            f"pos={self.items[nxt]['pos']} waits={self.items[nxt]['waits']}")
                    if state[nxt] == 0:
                        state[nxt] = 1
                        stack.append((nxt, iter(adj[nxt])))
                        advanced = True
                        break
                if not advanced:
                    state[node] = 2
                    stack.pop()

    def run_streams(self, block):
        for eng in ENGINES:
            getattr(block, eng)(lambda e, eng=eng: [f(e) for f in self.stream[eng]])


def build_program(stage):
    b = Builder()
    nc = b.nc
    SY, SC, GP, PE, DV, AC, CC = "SY", "SC", "GP", "PE", "DV", "AC", "CC"

    dram = lambda n, s, d, k="ExternalInput": nc.dram_tensor(n, list(s), d, kind=k).ap()
    X = dram("x0", [D, TPC], F32)
    WQ = dram("w_q", [D, 512], F32)
    WK = dram("w_k", [D, 128], F32)
    WV = dram("w_v", [D, 128], F32)
    WO = dram("w_o", [D, D], F32)
    WQT = dram("wq_t", [D, 64], F32)
    WRT = dram("wr_t", [64, 48], F32)
    AKT = dram("ak_t", [64, 1024], F32)
    AV = dram("av", [1024, D], F32)
    WVDT = dram("wvd_t", [D, 256], F32)
    WVUT = dram("wvu_t", [256, D], F32)
    WU1T = dram("wu1_t", [D, 1024], F32)
    WU2C = dram("wu2_c", [128, 8], F32)
    BU1C = dram("bu1_c", [128, 8], F32)
    SC3 = dram("sc3", [1, 8], F32)
    LOGREL = dram("logrel", [1, 1024], F32)
    ROPEC = dram("rope_c", [128, S], BF16)
    ROPES = dram("rope_s", [128, S], BF16)
    MASKS = dram("masks", [128, 2048], BF16)
    IDENT = dram("identt", [128, 128], BF16)
    EG = dram("eg", [128, 16], BF16)
    SEL4 = dram("sel4", [4, 512], F32)
    ONESK = dram("onesk", [1, 128], F32)
    E8K = dram("e8k", [1, 128], F32)
    ONES128 = dram("ones128", [128, 1], F32)
    CSTS = dram("csts", [128, 8], F32)
    IDXWO = dram("idx_wo", [128, 64], U32)

    Y = dram("y", [D, TPC], F32, "ExternalOutput")

    x_dram = dram("x_scr", [D, TPC], F32, "Internal")
    xh_in = dram("xh_in", [D, TPC], BF16, "Internal")
    xh_all = dram("xh_all", [NCORES * D, TPC], BF16, "Internal")
    q_scr = dram("q_scr", [512, S], BF16, "Internal")
    at_in = dram("at_in", [512, S], BF16, "Internal")
    at_all = dram("at_all", [D, S], BF16, "Internal")
    s_in = [dram(f"s{i}_in", [1, 1], F32, "Internal") for i in range(2)]
    s_out = [dram(f"s{i}_out", [8, 1], F32, "Internal") for i in range(2)]

    o = [b.SBUF_BASE]
    def P(name, shape, dtype):
        ap = b.at(name, shape, dtype, o[0])
        esz = 4 if dtype in (F32, U32) else 2
        o[0] += (int(np.prod(shape[1:])) * esz + 31) // 32 * 32
        return ap

    m_row = P("m_row", [1, 1024], F32)
    rstd_row = P("rstd_row", [1, 1024], F32)
    logv_row = P("logv_row", [1, 1024], F32)
    rt1 = P("rt1", [1, 1024], F32)
    rt2 = P("rt2", [1, 1024], F32)
    rt3 = P("rt3", [1, 1024], F32)
    gate_row = P("gate_row", [1, 1024], F32)
    logrel_row = P("logrel_row", [1, 1024], F32)
    c2_row = P("c2_row", [1, 1024], F32)
    gsc = P("gsc", [1, 16], F32)
    sc3 = P("sc3_sb", [1, 8], F32)
    csts = P("csts_sb", [128, 8], F32)
    onesk = P("onesk_sb", [1, 128], F32)
    e8k = P("e8k_sb", [1, 128], F32)
    ones128 = P("ones128_sb", [128, 1], F32)
    idxwo = P("idxwo_sb", [128, 64], U32)
    top8 = P("top8", [128, 8], F32)
    colA = P("colA", [128, 1], F32)
    colB = P("colB", [128, 1], F32)
    colC = P("colC", [128, 1], F32)
    colD = P("colD", [128, 1], F32)
    wu2c = P("wu2c_sb", [128, 8], BF16)
    bu1c = P("bu1c_sb", [128, 8], F32)
    eg = P("eg_sb", [128, 16], BF16)
    ident = P("ident_sb", [128, 128], BF16)
    sel4 = P("sel4_sb", [4, 512], F32)
    masks = P("masks_sb", [128, 2048], BF16)
    ak_bf = P("ak_bf", [64, 1024], BF16)
    rk_bf = P("rk_bf", [48, 1024], BF16)
    wr_bf = P("wr_bf", [64, 48], BF16)
    wq_bf = P("wq_bf", [128, 2048], BF16)
    WBASE = (o[0] + 63) // 64 * 64

    off = WBASE
    def nxt(nbytes):
        nonlocal off
        r = off
        off += (nbytes + 31) // 32 * 32
        return r
    O_XH = nxt(32 * 512 * 4)
    O_H = nxt(32 * 512 * 2)
    O_SQ0 = nxt(512 * 4)
    O_SQ1 = nxt(512 * 4)
    O_WU10 = nxt(1024 * 2)
    O_WU11 = nxt(1024 * 2)
    O_U1 = nxt(8 * 512 * 2)
    O_RT = nxt(1024 * 4)
    O_TT = nxt(1024 * 4)
    O_WTK = nxt(1024 * 4)
    O_WBF = nxt(1024 * 2)
    O_GX = nxt(512 * 4)
    O_G1 = nxt(512 * 4)
    O_G2 = nxt(512 * 4)
    O_BCA = nxt(512 * 4)
    O_BCB = nxt(512 * 4)
    O_AVB = nxt(2 * 512 * 2)
    O_WVDB = nxt(2 * 4 * 256 * 2)
    O_A1 = nxt(4 * 512 * 2)
    O_A2 = nxt(2 * 512 * 2)
    O_QRY = nxt(512 * 2)
    O_RQ = nxt(512 * 2)
    assert off <= b.SBUF_MAX, off

    xh = b.at("xh", [128, 32 * 512], F32, O_XH)
    h_sb = b.at("h_sb", [128, 32 * 512], BF16, O_H)
    sqb = [b.at("sq0", [128, 512], F32, O_SQ0), b.at("sq1", [128, 512], F32, O_SQ1)]
    wu1b = [b.at("wu1b0", [128, 1024], BF16, O_WU10), b.at("wu1b1", [128, 1024], BF16, O_WU11)]
    u1sb = b.at("u1sb", [128, 8 * 512], BF16, O_U1)
    Rt = b.at("Rt", [128, 1024], F32, O_RT)
    Tt = b.at("Tt", [128, 1024], F32, O_TT)
    wtok = b.at("wtok", [128, 1024], F32, O_WTK)
    wbf = b.at("wbf", [128, 1024], BF16, O_WBF)
    gx = b.at("gx", [128, 512], F32, O_GX)
    g1 = b.at("g1", [128, 512], F32, O_G1)
    g2 = b.at("g2", [128, 512], F32, O_G2)
    bcA = b.at("bcA", [128, 512], F32, O_BCA)
    bcB = b.at("bcB", [128, 512], F32, O_BCB)
    avb = [b.at("avb0", [128, 512], BF16, O_AVB), b.at("avb1", [128, 512], BF16, O_AVB + 1024)]
    wvdb = [b.at("wvdb0", [128, 4 * 256], BF16, O_WVDB),
            b.at("wvdb1", [128, 4 * 256], BF16, O_WVDB + 2048)]
    a1sb = b.at("a1sb", [128, 4 * 512], BF16, O_A1)
    a2sb = b.at("a2sb", [128, 2 * 512], BF16, O_A2)
    qryT = b.at("qryT", [64, 512], BF16, O_QRY)
    rqT = b.at("rqT", [48, 512], BF16, O_RQ)
    wT = b.at("wT", [128, 8 * 512], BF16, O_SQ0)
    wvub = b.at("wvub", [128, 2 * 4096], BF16, O_U1)

    import contextlib
    pctx = contextlib.ExitStack()
    PS = [pctx.enter_context(nc.psum_tensor(f"ps{i}", [128, 512], F32)) for i in range(8)]

    def dma(qeng, qsem, out, in_):
        return b.emit(qeng, lambda e, out=out, in_=in_: e.dma_start(out=out, in_=in_),
                      inc=qsem, n=16)

    def mm(out, lhsT, rhs, start, stop, inc=False):
        return b.emit("tensor",
                      lambda e, out=out, l=lhsT, r=rhs, s1=start, s2=stop:
                      e.matmul(out, l, r, start=s1, stop=s2),
                      inc=(PE if inc else None)) or b.snap(PE)

    def vop(fn, inc=True):
        return b.emit("vector", fn, inc=(DV if inc else None)) or b.snap(DV)

    def aop(fn, inc=True):
        return b.emit("scalar", fn, inc=(AC if inc else None)) or b.snap(AC)


    dma("sync", SY, csts, CSTS)
    dma("sync", SY, sc3, SC3)
    dma("sync", SY, onesk, ONESK)
    dma("sync", SY, e8k, E8K)
    dma("sync", SY, ones128, ONES128)
    dma("sync", SY, idxwo, IDXWO)
    cast = lambda out, in_: dma("gpsimd", "GP", out, in_)
    cast(wu2c, WU2C)
    dma("sync", SY, bu1c, BU1C)
    dma("sync", SY, eg, EG)
    dma("sync", SY, ident, IDENT)
    dma("sync", SY, sel4, SEL4)
    dma("sync", SY, masks, MASKS)
    consts_sy = dma("sync", SY, logrel_row, LOGREL)
    cast(ak_bf, AKT)
    cast(wr_bf, WRT)
    consts_gp = cast(wq_bf[:].rearrange("p (c n) -> p c n", n=64),
                     WQT.rearrange("(c p) n -> p c n", p=128))
    b.wait("tensor", "GP", consts_gp)

    b.wait("vector", GP, consts_gp)

    mm(PS[0][0:48, 0:512], wr_bf[:], ak_bf[:, 0:512], True, True)
    pe_rk = mm(PS[1][0:48, 0:512], wr_bf[:], ak_bf[:, 512:1024], True, True, inc=True)
    b.wait("vector", PE, pe_rk)
    vop(lambda e: e.tensor_scalar_mul(rk_bf[:, 0:512], PS[0][0:48, 0:512], SC_R), inc=False)
    dv_rk = vop(lambda e: e.tensor_scalar_mul(rk_bf[:, 512:1024], PS[1][0:48, 0:512], SC_R))

    state = {"prev_store": 0, "dv_rk": dv_rk, "dv_tail": 0}

    def ffn(src, dst, sin_d, sout_d):
        b.wait("tensor", DV, state["dv_tail"])
        b.wait("sync", SY, state["prev_store"])
        stat_pe = {}
        sq_pe = {0: 0, 1: 0}
        ac_sq = 0
        for h in (0, 1):
            if h == 1:
                b.wait("sync", PE, stat_pe[0])
                b.wait("sync", AC, ac_sq)
            ld = dma("sync", SY, xh[:].rearrange("p (c t) -> p c t", t=512),
                     src[:, 512 * h:512 * h + 512].rearrange("(c p) t -> p c t", p=128))
            b.wait("scalar", SY, ld)
            b.wait("tensor", SY, ld)
            if h == 0:
                b.wait("tensor", DV, state["dv_rk"])
            for c in range(NCH):
                if c >= 2 or h == 1:
                    b.wait("scalar", PE, sq_pe[c % 2])
                ac_sq = aop(lambda e, c=c: e.activation(
                    out=sqb[c % 2][:], in_=xh[:, 512 * c:512 * c + 512], func=AF.Square))
                b.wait("tensor", AC, ac_sq)
                mm(PS[0 + h][0:1, 0:512], ones128[:], xh[:, 512 * c:512 * c + 512],
                   c == 0, c == NCH - 1)
                sq_pe[c % 2] = mm(PS[2 + h][0:1, 0:512], ones128[:], sqb[c % 2][:],
                                  c == 0, c == NCH - 1, inc=True)
            stat_pe[h] = b.snap(PE)
        b.wait("vector", PE, stat_pe[1])
        for h in (0, 1):
            sl = slice(512 * h, 512 * h + 512)
            vop(lambda e, h=h, sl=sl: e.tensor_scalar_mul(m_row[0:1, sl], PS[0 + h][0:1, 0:512],
                                                          1.0 / D), inc=False)
        dv_m = vop(lambda e: e.engine_nop())
        b.wait("vector", DV, dv_m)
        vop(lambda e: e.tensor_tensor(out=rt1[:], in0=m_row[:], in1=m_row[:], op=ALU.mult),
            inc=False)
        for h in (0, 1):
            sl = slice(512 * h, 512 * h + 512)
            vop(lambda e, h=h, sl=sl: e.scalar_tensor_tensor(
                out=rt2[0:1, sl], in0=PS[2 + h][0:1, 0:512], scalar=1.0 / D, in1=rt1[0:1, sl],
                op0=ALU.mult, op1=ALU.subtract), inc=False)
        dv_v = vop(lambda e: e.engine_nop())
        b.wait("scalar", DV, dv_v)
        ac_ln = aop(lambda e: e.activation(out=rt3[:], in_=rt2[:], func=AF.Ln, bias=csts[0:1, 0:1]))
        b.wait("scalar", AC, ac_ln)
        ac_rstd = aop(lambda e: e.activation(out=rstd_row[:], in_=rt3[:], func=AF.Exp, scale=-0.5))
        b.wait("vector", AC, ac_ln)
        vop(lambda e: e.tensor_scalar_add(rt1[:], rt2[:], 1e-5), inc=False)
        dv_a = vop(lambda e: e.engine_nop())
        b.wait("vector", DV, dv_a)
        dv_b = vop(lambda e: e.reciprocal(rt1[:], rt1[:]))
        b.wait("vector", DV, dv_b)
        dv_vh = vop(lambda e: e.tensor_tensor(out=rt1[:], in0=rt2[:], in1=rt1[:], op=ALU.mult))
        b.wait("scalar", DV, dv_vh)
        ac_logv = aop(lambda e: e.activation(out=logv_row[:], in_=rt1[:], func=AF.Ln,
                                             bias=csts[0:1, 1:2]))
        b.wait("vector", AC, ac_logv)
        dv_part = vop(lambda e: e.reduce_sum(out=gsc[0:1, 0:1], in_=logv_row[:], axis=AX.X))
        b.wait("gpsimd", DV, dv_part)
        gp_s = dma("gpsimd", GP, sin_d, gsc[0:1, 0:1])
        b.wait("gpsimd", GP, gp_s)
        cc_done = b.emit("gpsimd", lambda e: e.collective_compute(
            "AllGather", ALU.bypass, ins=[sin_d.opt()], outs=[sout_d.opt()],
            replica_groups=[list(range(NCORES))]), inc=CC, n=1)
        b.wait("gpsimd", CC, cc_done)
        gp_ag = dma("gpsimd", GP, gsc[0:1, 4:12], sout_d.rearrange("a b -> b a"))
        b.wait("vector", GP, gp_ag)
        vop(lambda e: e.reduce_sum(out=gsc[0:1, 1:2], in_=gsc[0:1, 4:12], axis=AX.X), inc=False)
        dv_tot = vop(lambda e: e.engine_nop())
        b.wait("vector", DV, dv_tot)
        vop(lambda e: e.tensor_scalar(out=gsc[0:1, 2:3], in0=gsc[0:1, 1:2], scalar1=1.0 / S,
                                      scalar2=1e-6, op0=ALU.mult, op1=ALU.add), inc=False)
        dv_t2 = vop(lambda e: e.engine_nop())
        b.wait("vector", DV, dv_t2)
        dv_invm = vop(lambda e: e.reciprocal(gsc[0:1, 2:3], gsc[0:1, 2:3]))

        wu1_pe = {0: 0, 1: 0}
        av_pe = {0: 0, 1: 0}
        wvd_pe = {0: 0, 1: 0}
        st = 0
        for hi, h in enumerate((1, 0)):
            sl = slice(512 * h, 512 * h + 512)
            if hi == 1:
                b.wait("tensor", DV, dv_ep_prev)
                b.wait("sync", SY, st)
                ld = dma("sync", SY, xh[:].rearrange("p (c t) -> p c t", t=512),
                         src[:, sl].rearrange("(c p) t -> p c t", p=128))
                b.wait("vector", SY, ld)
                b.wait("tensor", SY, ld)
            b.wait("tensor", AC, ac_rstd)
            mm(PS[5][:, 0:512], onesk[:], m_row[0:1, sl], True, True)
            pe_bc = mm(PS[6][:, 0:512], onesk[:], rstd_row[0:1, sl], True, True, inc=True)
            b.wait("vector", PE, pe_bc)
            vop(lambda e: e.tensor_copy(out=bcA[:], in_=PS[5][:, 0:512]), inc=False)
            dv_bc = vop(lambda e: e.tensor_copy(out=bcB[:], in_=PS[6][:, 0:512]))
            for c in range(NCH):
                vop(lambda e, c=c: e.tensor_tensor(out=gx[:], in0=xh[:, 512 * c:512 * c + 512],
                                                   in1=bcA[:], op=ALU.subtract), inc=False)
                vop(lambda e, c=c: e.tensor_tensor(out=h_sb[:, 512 * c:512 * c + 512],
                                                   in0=gx[:], in1=bcB[:], op=ALU.mult),
                    inc=(c == NCH - 1))
            dv_h = b.snap(DV)
            b.wait("tensor", DV, dv_h)
            for c in range(NCH):
                cm = c % 2
                b.wait("gpsimd", PE, wu1_pe[cm])
                ldw = cast(wu1b[cm][:], WU1T[128 * c:128 * c + 128, :])
                b.wait("tensor", GP, ldw)
                for u in range(8):
                    wu1_pe[cm] = mm(PS[u][:, 0:512], wu1b[cm][:, 128 * u:128 * u + 128],
                                    h_sb[:, 512 * c:512 * c + 512], c == 0, c == NCH - 1,
                                    inc=(u == 7))
            pe_u1 = b.snap(PE)
            b.wait("vector", PE, pe_u1)
            ac2 = 0
            for u in range(8):
                b.wait("vector", AC, ac2)
                dv0 = vop(lambda e, u=u: e.tensor_scalar(
                    out=gx[:], in0=PS[u][:, 0:512], scalar1=bu1c[:, u:u + 1],
                    scalar2=None, op0=ALU.add))
                b.wait("scalar", DV, dv0)
                ac1 = aop(lambda e: e.activation(out=g1[:], in_=gx[:], func=AF.Square))
                b.wait("vector", AC, ac1)
                vop(lambda e: e.tensor_scalar(out=g1[:], in0=g1[:], scalar1=GCA, scalar2=GC,
                                              op0=ALU.mult, op1=ALU.add), inc=False)
                dv1 = vop(lambda e: e.tensor_tensor(out=g1[:], in0=g1[:], in1=gx[:], op=ALU.mult))
                b.wait("scalar", DV, dv1)
                ac2 = aop(lambda e: e.activation(out=g2[:], in_=g1[:], func=AF.Tanh))
                b.wait("vector", AC, ac2)
                vop(lambda e: e.tensor_scalar(out=g2[:], in0=g2[:], scalar1=0.5, scalar2=0.5,
                                              op0=ALU.mult, op1=ALU.add), inc=False)
                dv_prev = vop(lambda e, u=u: e.tensor_tensor(
                    out=u1sb[:, 512 * u:512 * u + 512], in0=g2[:], in1=gx[:], op=ALU.mult))
            dv_gelu = dv_prev
            b.wait("tensor", DV, dv_gelu)
            for u in range(8):
                mm(PS[0][0:1, 0:512], wu2c[:, u:u + 1], u1sb[:, 512 * u:512 * u + 512],
                   u == 0, u == 7, inc=(u == 7))
            pe_lr = b.snap(PE)
            b.wait("vector", PE, pe_lr)
            vop(lambda e: e.tensor_scalar(out=rt1[0:1, 0:512], in0=PS[0][0:1, 0:512],
                                          scalar1=sc3[0:1, 0:1], scalar2=None, op0=ALU.add),
                inc=False)
            dv_l = vop(lambda e: e.engine_nop())
            b.wait("scalar", DV, dv_l)
            ac_sg = aop(lambda e: e.activation(out=rt2[0:1, 0:512], in_=rt1[0:1, 0:512],
                                               func=AF.Tanh, scale=0.5))
            b.wait("vector", AC, ac_sg)
            b.wait("vector", DV, dv_invm)
            vop(lambda e: e.tensor_scalar(out=rt2[0:1, 0:512], in0=rt2[0:1, 0:512], scalar1=1.25,
                                          scalar2=1.25, op0=ALU.mult, op1=ALU.add), inc=False)
            vop(lambda e, sl=sl: e.tensor_scalar(out=rt3[0:1, 0:512], in0=logv_row[0:1, sl],
                                                 scalar1=gsc[0:1, 2:3], scalar2=0.5,
                                                 op0=ALU.mult, op1=ALU.mult), inc=False)
            vop(lambda e: e.tensor_tensor(out=rt2[0:1, 0:512], in0=rt2[0:1, 0:512],
                                          in1=rt3[0:1, 0:512], op=ALU.add), inc=False)
            vop(lambda e: e.tensor_scalar(out=rt2[0:1, 0:512], in0=rt2[0:1, 0:512], scalar1=0.0,
                                          scalar2=5.0, op0=ALU.max, op1=ALU.min), inc=False)
            vop(lambda e: e.tensor_scalar(out=rt2[0:1, 0:512], in0=rt2[0:1, 0:512], scalar1=0.5,
                                          scalar2=1.0 / 1.5, op0=ALU.subtract, op1=ALU.mult),
                inc=False)
            vop(lambda e: e.tensor_scalar(out=rt2[0:1, 0:512], in0=rt2[0:1, 0:512], scalar1=0.0,
                                          scalar2=1.0, op0=ALU.max, op1=ALU.min), inc=False)
            vop(lambda e: e.tensor_scalar(out=rt2[0:1, 0:512], in0=rt2[0:1, 0:512],
                                          scalar1=sc3[0:1, 1:2], scalar2=sc3[0:1, 2:3],
                                          op0=ALU.mult, op1=ALU.add), inc=False)
            dv_z = vop(lambda e: e.engine_nop())
            b.wait("scalar", DV, dv_z)
            ac_g = aop(lambda e: e.activation(out=rt3[0:1, 0:512], in_=rt2[0:1, 0:512],
                                              func=AF.Tanh, scale=0.5))
            b.wait("vector", AC, ac_g)
            vop(lambda e: e.tensor_scalar(out=rt3[0:1, 0:512], in0=rt3[0:1, 0:512], scalar1=0.5,
                                          scalar2=0.5, op0=ALU.mult, op1=ALU.add), inc=False)
            dv_g1 = vop(lambda e: e.engine_nop())
            b.wait("vector", DV, dv_g1)
            vop(lambda e: e.tensor_scalar(out=rt2[0:1, 0:512], in0=rt3[0:1, 0:512], scalar1=0.05,
                                          scalar2=None, op0=ALU.is_ge), inc=False)
            vop(lambda e: e.tensor_tensor(out=rt3[0:1, 0:512], in0=rt3[0:1, 0:512],
                                          in1=rt2[0:1, 0:512], op=ALU.mult), inc=False)
            dv_gate = vop(lambda e, sl=sl: e.tensor_scalar_mul(gate_row[0:1, sl],
                                                               rt3[0:1, 0:512], 0.5))

            for c in range(NCH):
                mm(PS[7][0:64, 0:512], wq_bf[:, 64 * c:64 * c + 64],
                   h_sb[:, 512 * c:512 * c + 512], c == 0, c == NCH - 1, inc=(c == NCH - 1))
            pe_q = b.snap(PE)
            b.wait("vector", PE, pe_q)
            dv_q = vop(lambda e: e.tensor_copy(out=qryT[:], in_=PS[7][0:64, 0:512]))
            b.wait("tensor", DV, dv_q)
            pe_rq = mm(PS[6][0:48, 0:512], wr_bf[:], qryT[:], True, True, inc=True)
            b.wait("vector", PE, pe_rq)
            dv_rq = vop(lambda e: e.tensor_scalar_mul(rqT[:], PS[6][0:48, 0:512], SC_R))
            b.wait("tensor", DV, dv_rq)
            b.wait("tensor", DV, dv_gate)

            dv_cd = 0
            pe_tr_last = 0
            for t4 in range(4):
                ts0 = 128 * t4
                if t4 > 0:
                    b.wait("tensor", DV, dv_cd)
                mm(PS[0][:, 0:512], rqT[:, ts0:ts0 + 128], rk_bf[:, 0:512], True, False)
                mm(PS[0][:, 0:512], onesk[:], logrel_row[0:1, 0:512], False, True)
                mm(PS[1][:, 0:512], rqT[:, ts0:ts0 + 128], rk_bf[:, 512:1024], True, False)
                mm(PS[1][:, 0:512], onesk[:], logrel_row[0:1, 512:1024], False, True)
                mm(PS[2][:, 0:512], qryT[:, ts0:ts0 + 128], ak_bf[:, 0:512], True, False)
                mm(PS[2][:, 0:512], e8k[:], logrel_row[0:1, 0:512], False, True)
                mm(PS[3][:, 0:512], qryT[:, ts0:ts0 + 128], ak_bf[:, 512:1024], True, False)
                mm(PS[3][:, 0:512], e8k[:], logrel_row[0:1, 512:1024], False, True)
                pe_r = mm(PS[4][:, 0:1], gate_row[0:1, 512 * h + ts0:512 * h + ts0 + 128],
                          onesk[0:1, 0:1], True, True, inc=True)
                b.wait("vector", PE, pe_r)
                vop(lambda e: e.tensor_copy(out=Rt[:, 0:512], in_=PS[0][:, 0:512]), inc=False)
                dv_R = vop(lambda e: e.tensor_copy(out=Rt[:, 512:1024], in_=PS[1][:, 0:512]))
                b.wait("vector", DV, dv_R)
                dv_m8 = vop(lambda e: e.max(out=top8[:], in_=Rt[:]))
                b.wait("vector", DV, dv_m8)
                vop(lambda e: e.tensor_scalar(out=wtok[:], in0=Rt[:], scalar1=top8[:, 7:8],
                                              scalar2=None, op0=ALU.is_lt), inc=False)
                vop(lambda e: e.scalar_tensor_tensor(out=Tt[:, 0:512], in0=wtok[:, 0:512],
                                                     scalar=-1e30, in1=PS[2][:, 0:512],
                                                     op0=ALU.mult, op1=ALU.add), inc=False)
                dv_T = vop(lambda e: e.scalar_tensor_tensor(
                    out=Tt[:, 512:1024], in0=wtok[:, 512:1024], scalar=-1e30,
                    in1=PS[3][:, 0:512], op0=ALU.mult, op1=ALU.add))
                b.wait("vector", DV, dv_T)
                dv_mx = vop(lambda e: e.reduce_max(out=colA[:], in_=Tt[:], axis=AX.X))
                b.wait("vector", DV, dv_mx)
                dv_nb = vop(lambda e: e.tensor_scalar_mul(colB[:], colA[:], -0.125))
                b.wait("scalar", DV, dv_nb)
                ac_e = aop(lambda e: e.activation(out=wtok[:], in_=Tt[:], func=AF.Exp,
                                                  bias=colB[:], scale=0.125))
                b.wait("vector", AC, ac_e)
                dv_s8 = vop(lambda e: e.reduce_sum(out=colC[:], in_=wtok[:], axis=AX.X))
                b.wait("vector", DV, dv_s8)
                dv_r8 = vop(lambda e: e.reciprocal(colC[:], colC[:]))
                b.wait("vector", DV, dv_r8)
                dv_cd = vop(lambda e: e.tensor_tensor(out=colD[:], in0=PS[4][:, 0:1],
                                                      in1=colC[:], op=ALU.mult))
                b.wait("vector", DV, dv_cd)
                b.wait("vector", PE, pe_tr_last)
                dv_wb = vop(lambda e: e.tensor_scalar(out=wbf[:], in0=wtok[:], scalar1=colD[:],
                                                      scalar2=None, op0=ALU.mult))
                b.wait("tensor", DV, dv_wb)
                pst = PS[5][:, 0:512].bitcast(BF16)
                dv_tr = 0
                for n in range(8):
                    if n > 0:
                        b.wait("tensor", DV, dv_tr)
                    pe_tr_last = b.emit("tensor",
                                        lambda e, n=n, pst=pst: e.transpose(
                                            pst[:, 0:128], wbf[:, 128 * n:128 * n + 128], ident[:]),
                                        inc=PE)
                    b.wait("vector", PE, pe_tr_last)
                    dv_tr = vop(lambda e, n=n, ts0=ts0, pst=pst: e.tensor_copy(
                        out=wT[:, 512 * n + ts0:512 * n + ts0 + 128], in_=pst[:, 0:128]))
            dv_wT = dv_tr

            b.wait("tensor", DV, dv_wT)
            for g in range(8):
                gm = g % 2
                b.wait("gpsimd", PE, wvd_pe[gm])
                ldw = cast(wvdb[gm][:].rearrange("p (a n) -> p a n", a=4),
                           WVDT[512 * g:512 * g + 512, :].rearrange("(a p) n -> p a n", p=128))
                for n in range(8):
                    nm = n % 2
                    b.wait("gpsimd", PE, av_pe[nm])
                    lda = cast(avb[nm][:], AV[128 * n:128 * n + 128, 512 * g:512 * g + 512])
                    b.wait("tensor", GP, lda)
                    for u4 in range(4):
                        av_pe[nm] = mm(PS[u4][:, 0:512], avb[nm][:, 128 * u4:128 * u4 + 128],
                                       wT[:, 512 * n:512 * n + 512], n == 0, n == 7,
                                       inc=(u4 == 3))
                pe_a1 = b.snap(PE)
                b.wait("vector", PE, pe_a1)
                for u4 in range(4):
                    vop(lambda e, u4=u4: e.tensor_copy(out=a1sb[:, 512 * u4:512 * u4 + 512],
                                                       in_=PS[u4][:, 0:512]),
                        inc=(u4 == 3))
                dv_a1 = b.snap(DV)
                b.wait("tensor", DV, dv_a1)
                b.wait("tensor", GP, ldw)
                for u4 in range(4):
                    for v2 in range(2):
                        wvd_pe[gm] = mm(
                            PS[4 + v2][:, 0:512],
                            wvdb[gm][:, 256 * u4 + 128 * v2:256 * u4 + 128 * v2 + 128],
                            a1sb[:, 512 * u4:512 * u4 + 512],
                            (g == 0 and u4 == 0), (g == 7 and u4 == 3),
                            inc=(u4 == 3 and v2 == 1))
            pe_a2 = b.snap(PE)
            b.wait("vector", PE, pe_a2)
            vop(lambda e: e.tensor_copy(out=a2sb[:, 0:512], in_=PS[4][:, 0:512]), inc=False)
            dv_a2 = vop(lambda e: e.tensor_copy(out=a2sb[:, 512:1024], in_=PS[5][:, 0:512]))
            vop(lambda e, sl=sl: e.tensor_tensor(out=c2_row[0:1, sl], in0=m_row[0:1, sl],
                                                 in1=rstd_row[0:1, sl], op=ALU.mult), inc=False)
            dv_c2a = vop(lambda e: e.engine_nop())
            b.wait("vector", DV, dv_c2a)
            vop(lambda e, sl=sl: e.tensor_scalar_mul(c2_row[0:1, sl], c2_row[0:1, sl], -0.5),
                inc=False)
            dv_c1 = vop(lambda e, sl=sl: e.tensor_scalar(
                out=rt1[0:1, 0:512], in0=rstd_row[0:1, sl], scalar1=0.5, scalar2=1.0,
                op0=ALU.mult, op1=ALU.add))
            b.wait("tensor", DV, dv_c1)
            pe_c1b = mm(PS[6][:, 0:512], onesk[:], rt1[0:1, 0:512], True, True, inc=True)
            b.wait("vector", PE, pe_c1b)
            dv_bc1 = vop(lambda e: e.tensor_copy(out=bcA[:], in_=PS[6][:, 0:512]))
            ldu = cast(wvub[:].rearrange("p (a n) -> p a n", a=2),
                       WVUT.rearrange("(a p) n -> p a n", p=128))
            b.wait("tensor", GP, ldu)
            b.wait("tensor", DV, dv_a2)
            b.wait("tensor", DV, dv_bc1)
            ep_dv = {6: 0, 7: 0}
            for f in range(NCH):
                fb = 6 + (f % 2)
                b.wait("tensor", DV, ep_dv[fb])
                for v2 in range(2):
                    mm(PS[fb][:, 0:512], wvub[:, 4096 * v2 + 128 * f:4096 * v2 + 128 * f + 128],
                       a2sb[:, 512 * v2:512 * v2 + 512], v2 == 0, False)
                pe_f = mm(PS[fb][:, 0:512], onesk[:], c2_row[0:1, sl], False, True, inc=True)
                b.wait("vector", PE, pe_f)
                vop(lambda e, f=f: e.tensor_tensor(out=gx[:], in0=xh[:, 512 * f:512 * f + 512],
                                                   in1=bcA[:], op=ALU.mult), inc=False)
                ep_dv[fb] = vop(lambda e, f=f, fb=fb: e.tensor_tensor(
                    out=xh[:, 512 * f:512 * f + 512], in0=gx[:], in1=PS[fb][:, 0:512],
                    op=ALU.add))
            dv_ep = b.snap(DV)
            dv_ep_prev = dv_ep
            b.wait("sync", DV, dv_ep)
            st = dma("sync", SY, dst[:, sl].rearrange("(c p) t -> p c t", p=128),
                     xh[:].rearrange("p (c t) -> p c t", t=512))
        state["dv_tail"] = dv_ep_prev
        state["prev_store"] = st
        return st


    def attention(src, dst):
        b.wait("tensor", DV, state["dv_tail"])
        AB = WBASE
        K_sb = b.at("K_sb", [128, S], BF16, AB)
        V_sb = b.at("V_sb", [128, S], BF16, AB + 16384)
        xq = [b.at("xq0", [128, 32 * 512], BF16, AB + 32768),
              b.at("xq1", [128, 32 * 512], BF16, AB + 65536)]
        wqkv = b.at("wqkv", [128, 32 * 768], BF16, AB + 98304)
        rpc = [b.at("rpc0", [128, 512], BF16, AB + 147456),
               b.at("rpc1", [128, 512], BF16, AB + 148480)]
        rps = [b.at("rps0", [128, 512], BF16, AB + 149504),
               b.at("rps1", [128, 512], BF16, AB + 150528)]
        racc = b.at("racc", [128, 512], F32, AB + 151552)
        rtmp = b.at("rtmp", [128, 512], F32, AB + 153600)
        qst = [b.at("qst0", [128, 512], BF16, AB + 155648),
               b.at("qst1", [128, 512], BF16, AB + 156672)]
        vtmp = b.at("vtmp", [128, 512], BF16, AB + 157696)
        ML = AB + 32768
        q_tile = b.at("q_tile", [128, 4 * 512], BF16, ML)
        probs = [b.at(f"probs{i}", [128, 512], BF16, ML + 4096 + 1024 * i) for i in range(2)]
        probm = [b.at(f"probm{i}", [128, 512], BF16, ML + 6144 + 1024 * i) for i in range(2)]
        attn_t = b.at("attn_t", [128, 4 * 512], BF16, ML + 8192)
        s_sb = b.at("s_sb", [4, 512], F32, ML + 12288)
        rcp = b.at("rcp", [4, 512], F32, ML + 14336)
        bcs = [b.at("bcs0", [128, 512], F32, ML + 16384),
               b.at("bcs1", [128, 512], F32, ML + 18432)]
        xh_w = b.at("xh_w", [128, 32 * 512], F32, AB)
        at_half = b.at("at_half", [128, 32 * 512], BF16, AB + 65536)
        wob = [b.at("wob0", [128, 4 * 512], BF16, AB + 98304),
               b.at("wob1", [128, 4 * 512], BF16, AB + 102400)]

        sq_pe = {0: 0, 1: 0}
        ac_sq = 0
        stat_pe = 0
        xst = 0
        dv_v_prev = 0
        b.wait("sync", SY, state["prev_store"])
        for h in (0, 1):
            if h == 1:
                b.wait("sync", PE, stat_pe)
                b.wait("sync", AC, ac_sq)
                b.wait("sync", DV, dv_hd)
            ld = dma("sync", SY, xh[:].rearrange("p (c t) -> p c t", t=512),
                     src[:, 512 * h:512 * h + 512].rearrange("(c p) t -> p c t", p=128))
            b.wait("scalar", SY, ld)
            b.wait("tensor", SY, ld)
            b.wait("tensor", DV, dv_v_prev if h == 1 else state["dv_tail"])
            for c in range(NCH):
                if c >= 2 or h == 1:
                    b.wait("scalar", PE, sq_pe[c % 2])
                ac_sq = aop(lambda e, c=c: e.activation(
                    out=sqb[c % 2][:], in_=xh[:, 512 * c:512 * c + 512], func=AF.Square))
                b.wait("tensor", AC, ac_sq)
                mm(PS[0][0:1, 0:512], ones128[:], xh[:, 512 * c:512 * c + 512],
                   c == 0, c == NCH - 1)
                sq_pe[c % 2] = mm(PS[2][0:1, 0:512], ones128[:], sqb[c % 2][:],
                                  c == 0, c == NCH - 1, inc=True)
            stat_pe = b.snap(PE)
            sl = slice(512 * h, 512 * h + 512)
            b.wait("vector", PE, stat_pe)
            vop(lambda e, sl=sl: e.tensor_scalar_mul(m_row[0:1, sl], PS[0][0:1, 0:512],
                                                     1.0 / D), inc=False)
            dv_m = vop(lambda e: e.engine_nop())
            b.wait("vector", DV, dv_m)
            vop(lambda e, sl=sl: e.tensor_tensor(out=rt1[0:1, 0:512], in0=m_row[0:1, sl],
                                                 in1=m_row[0:1, sl], op=ALU.mult), inc=False)
            dv_v = vop(lambda e: e.scalar_tensor_tensor(
                out=rt2[0:1, 0:512], in0=PS[2][0:1, 0:512], scalar=1.0 / D,
                in1=rt1[0:1, 0:512], op0=ALU.mult, op1=ALU.subtract))
            dv_v_prev = dv_v
            b.wait("scalar", DV, dv_v)
            ac_ln = aop(lambda e: e.activation(out=rt3[0:1, 0:512], in_=rt2[0:1, 0:512],
                                               func=AF.Ln, bias=csts[0:1, 0:1]))
            b.wait("scalar", AC, ac_ln)
            ac_rstd = aop(lambda e, sl=sl: e.activation(out=rstd_row[0:1, sl],
                                                        in_=rt3[0:1, 0:512],
                                                        func=AF.Exp, scale=-0.5))
            b.wait("tensor", AC, ac_rstd)
            b.wait("tensor", DV, dv_v)
            mm(PS[5][:, 0:512], onesk[:], m_row[0:1, sl], True, True)
            pe_bc = mm(PS[6][:, 0:512], onesk[:], rstd_row[0:1, sl], True, True, inc=True)
            b.wait("vector", PE, pe_bc)
            vop(lambda e: e.tensor_copy(out=bcA[:], in_=PS[5][:, 0:512]), inc=False)
            vop(lambda e: e.tensor_copy(out=bcB[:], in_=PS[6][:, 0:512]), inc=False)
            for c in range(NCH):
                vop(lambda e, c=c: e.tensor_tensor(out=gx[:], in0=xh[:, 512 * c:512 * c + 512],
                                                   in1=bcA[:], op=ALU.subtract), inc=False)
                vop(lambda e, c=c: e.tensor_tensor(out=h_sb[:, 512 * c:512 * c + 512],
                                                   in0=gx[:], in1=bcB[:], op=ALU.mult),
                    inc=(c == NCH - 1))
            dv_hd = b.snap(DV)
            b.wait("sync", DV, dv_hd)
            xst = dma("sync", SY, xh_in[:, sl].rearrange("(c p) t -> p c t", p=128),
                      h_sb[:].rearrange("p (c t) -> p c t", t=512))
        b.wait("gpsimd", SY, xst)
        cc2 = b.emit("gpsimd", lambda e: e.collective_compute(
            "AllGather", ALU.bypass, ins=[xh_in.opt()], outs=[xh_all.opt()],
            replica_groups=[list(range(NCORES))]), inc=CC, n=1)

        wq_ld = cast(wqkv[:].rearrange("p (c f) -> p c f", f=768)[:, :, 0:512],
                     WQ.rearrange("(c p) f -> p c f", p=128))
        wk_ld = cast(wqkv[:].rearrange("p (c f) -> p c f", f=768)[:, :, 512:640],
                     WK.rearrange("(c p) f -> p c f", p=128))
        wv_ld = cast(wqkv[:].rearrange("p (c f) -> p c f", f=768)[:, :, 640:768],
                     WV.rearrange("(c p) f -> p c f", p=128))
        b.wait("tensor", GP, wv_ld)
        b.wait("sync", CC, cc2)
        xq_pe = {0: 0, 1: 0}
        rp_dv = {0: 0, 1: 0}
        qst_sy = {0: 0, 1: 0}
        dv_q_tail = 0
        ac_qk = 0
        for tb in range(16):
            tm = tb % 2
            rank, lt = tb // 2, (tb % 2) * 512
            b.wait("sync", PE, xq_pe[tm])
            lx = dma("sync", SY, xq[tm][:].rearrange("p (c t) -> p c t", t=512),
                     xh_all[D * rank:D * rank + D, lt:lt + 512]
                     .rearrange("(c p) t -> p c t", p=128))
            b.wait("sync", DV, rp_dv[tm])
            lrc = dma("sync", SY, rpc[tm][:], ROPEC[:, 512 * tb:512 * tb + 512])
            lrs = dma("sync", SY, rps[tm][:], ROPES[:, 512 * tb:512 * tb + 512])
            b.wait("tensor", SY, lrs)
            b.wait("tensor", DV, dv_q_tail)
            for c in range(NCH):
                for f in range(6):
                    xq_pe[tm] = mm(PS[f][:, 0:512],
                                   wqkv[:, 768 * c + 128 * f:768 * c + 128 * f + 128],
                                   xq[tm][:, 512 * c:512 * c + 512],
                                   c == 0, c == NCH - 1, inc=(f == 5))
            pe_qkv = b.snap(PE)
            b.wait("vector", PE, pe_qkv)
            for fi in range(5):
                dst_is_k = (fi == 4)
                qm = fi % 2
                if not dst_is_k:
                    b.wait("vector", SY, qst_sy[qm])
                vop(lambda e, fi=fi, tm=tm: e.tensor_tensor(
                    out=racc[:], in0=PS[fi][:, 0:512], in1=rpc[tm][:], op=ALU.mult),
                    inc=False)
                vop(lambda e, fi=fi, tm=tm: e.tensor_tensor(
                    out=rtmp[0:64, :], in0=PS[fi][64:128, 0:512], in1=rps[tm][0:64, :],
                    op=ALU.mult), inc=False)
                vop(lambda e, fi=fi, tm=tm: e.tensor_tensor(
                    out=rtmp[64:128, :], in0=PS[fi][0:64, 0:512], in1=rps[tm][64:128, :],
                    op=ALU.mult), inc=False)
                if dst_is_k:
                    dv_ro = vop(lambda e, tb=tb: e.tensor_tensor(
                        out=K_sb[:, 512 * tb:512 * tb + 512], in0=racc[:], in1=rtmp[:],
                        op=ALU.add))
                else:
                    dv_ro = vop(lambda e, qm=qm: e.tensor_tensor(
                        out=qst[qm][:], in0=racc[:], in1=rtmp[:], op=ALU.add))
                    b.wait("sync", DV, dv_ro)
                    qst_sy[qm] = dma("sync", SY, q_scr[128 * fi:128 * fi + 128,
                                                      512 * tb:512 * tb + 512], qst[qm][:])
            dv_vt = vop(lambda e: e.tensor_copy(out=vtmp[:], in_=PS[5][:, 0:512]))
            b.wait("tensor", DV, dv_vt)
            pst7 = PS[7][:, 0:512].bitcast(BF16)
            dv_tr = 0
            for ts4 in range(4):
                if ts4 > 0:
                    b.wait("tensor", DV, dv_tr)
                pe_tr = b.emit("tensor", lambda e, ts4=ts4, pst7=pst7: e.transpose(
                    pst7[:, 0:128], vtmp[:, 128 * ts4:128 * ts4 + 128], ident[:]), inc=PE)
                b.wait("vector", PE, pe_tr)
                dv_tr = vop(lambda e, tb=tb, ts4=ts4, pst7=pst7: e.tensor_copy(
                    out=V_sb[:, 512 * tb + 128 * ts4:512 * tb + 128 * ts4 + 128],
                    in_=pst7[:, 0:128]))
            rp_dv[tm] = dv_tr
            dv_q_tail = dv_tr
        dv_qkv_done = dv_q_tail
        qspill_all = b.snap(SY)

        b.wait("tensor", DV, dv_qkv_done)
        ac_exp = {0: 0, 1: 0}
        probs_pe = {0: 0, 1: 0}
        probs_dv = {0: 0, 1: 0}
        dv_at = 0
        dv_s = 0
        dv_bcg = 0
        at_sy = 0
        pe_tail = 0
        for t in range(16):
            b.wait("sync", SY, qspill_all)
            b.wait("sync", PE, pe_tail)
            lq = dma("sync", SY, q_tile[:].rearrange("p (g t) -> p g t", t=512),
                     q_scr[:, 512 * t:512 * t + 512].rearrange("(g p) t -> p g t", p=128))
            b.wait("tensor", SY, lq)
            nkb = 4 * (t + 1)
            it = 0
            for kb in range(nkb):
                for g in range(4):
                    sbi = it % 2
                    it += 1
                    b.wait("tensor", AC, ac_exp[sbi])
                    pe_qk = mm(PS[4 + sbi][:, 0:512], K_sb[:, 128 * kb:128 * kb + 128],
                               q_tile[:, 512 * g:512 * g + 512], True, True, inc=True)
                    b.wait("scalar", PE, pe_qk)
                    b.wait("scalar", PE, probs_pe[sbi])
                    b.wait("scalar", DV, probs_dv[sbi])
                    ac_exp[sbi] = aop(lambda e, sbi=sbi: e.activation(
                        out=probs[sbi][:], in_=PS[4 + sbi][:, 0:512], func=AF.Exp,
                        bias=csts[:, 2:3], scale=SC_ATT))
                    j = kb - 4 * t
                    if j >= 0:
                        b.wait("vector", AC, ac_exp[sbi])
                        b.wait("vector", PE, probs_pe[sbi])
                        probs_dv[sbi] = vop(lambda e, sbi=sbi, j=j: e.tensor_tensor(
                            out=probm[sbi][:], in0=probs[sbi][:],
                            in1=masks[:, 512 * j:512 * j + 512], op=ALU.mult))
                        b.wait("tensor", DV, probs_dv[sbi])
                        px = probm[sbi]
                    else:
                        b.wait("tensor", AC, ac_exp[sbi])
                        px = probs[sbi]
                    if kb == 0:
                        if g == 0:
                            b.wait("tensor", DV, dv_s)
                        b.wait("tensor", DV, dv_at)
                    mm(PS[6][0:4, 0:512], eg[:, 4 * g:4 * g + 4], px[:],
                       kb == 0 and g == 0, kb == nkb - 1 and g == 3,
                       inc=(kb == nkb - 1 and g == 3))
                    probs_pe[sbi] = mm(PS[g][:, 0:512], V_sb[:, 128 * kb:128 * kb + 128],
                                       px[:], kb == 0, kb == nkb - 1, inc=True)
            pe_tail = b.snap(PE)
            b.wait("vector", PE, pe_tail)
            b.wait("vector", SY, at_sy)
            dv_s = vop(lambda e: e.tensor_copy(out=s_sb[:], in_=PS[6][0:4, 0:512]))
            b.wait("vector", DV, dv_s)
            dv_rcp = vop(lambda e: e.reciprocal(rcp[:], s_sb[:]))
            b.wait("tensor", DV, dv_rcp)
            dv_bcg = 0
            for g in range(4):
                if g > 0:
                    b.wait("tensor", DV, dv_bcg)
                pe_bc = mm(PS[7][:, 0:512], sel4[:, 128 * g:128 * g + 128], rcp[:],
                           True, True, inc=True)
                b.wait("vector", PE, pe_bc)
                dv_bcg = vop(lambda e, g=g: e.tensor_copy(out=bcs[g % 2][:],
                                                          in_=PS[7][:, 0:512]))
                b.wait("vector", DV, dv_bcg)
                vop(lambda e, g=g: e.tensor_tensor(
                    out=attn_t[:, 512 * g:512 * g + 512], in0=PS[g][:, 0:512],
                    in1=bcs[g % 2][:], op=ALU.mult), inc=(g == 3))
            dv_at = b.snap(DV)
            b.wait("sync", DV, dv_at)
            for g in range(4):
                at_sy = dma("sync", SY, at_in[128 * g:128 * g + 128, 512 * t:512 * t + 512],
                            attn_t[:, 512 * g:512 * g + 512])

        b.wait("gpsimd", SY, at_sy)
        cc3 = b.emit("gpsimd", lambda e: e.collective_compute(
            "AllGather", ALU.bypass, ins=[at_in.opt()], outs=[at_all.opt()],
            replica_groups=[list(range(NCORES))]), inc=CC, n=1)
        at_view = at_all.rearrange("e (s t) -> (e s) t", t=512)
        wo_pe = {0: 0, 1: 0}
        st = 0
        dv_wo_prev = dv_at
        pe_gth_gate = 0
        for h in (0, 1):
            sl = slice(512 * h, 512 * h + 512)
            b.wait("tensor", DV, dv_wo_prev)
            b.wait("sync", SY, max(st, state["prev_store"]))
            lxw = dma("sync", SY, xh_w[:].rearrange("p (c t) -> p c t", t=512),
                      src[:, sl].rearrange("(c p) t -> p c t", p=128))
            b.wait("gpsimd", CC, cc3)
            b.wait("gpsimd", PE, pe_gth_gate)
            gth = 0
            for ce in range(NCH):
                gth = b.emit("gpsimd", lambda e, ce=ce, h=h: e.indirect_dma_start(
                    out=at_half[:, 512 * ce:512 * ce + 512], out_offset=None,
                    in_=at_view,
                    in_offset=IndirectOffsetOnAxis(ap=idxwo[:, 2 * ce + h:2 * ce + h + 1],
                                                   axis=0)), inc=GP, n=16)
            b.wait("tensor", GP, gth)
            b.wait("vector", SY, lxw)
            ep_prev = {0: 0, 1: 0, 2: 0, 3: 0}
            for fg in range(8):
                for f4 in range(4):
                    b.wait("tensor", DV, ep_prev[f4])
                for em in range(8):
                    emm = em % 2
                    b.wait("gpsimd", PE, wo_pe[emm])
                    lw = cast(wob[emm][:].rearrange("p (a f) -> p a f", a=4),
                              WO[512 * em:512 * em + 512, 512 * fg:512 * fg + 512]
                              .rearrange("(a p) f -> p a f", p=128))
                    b.wait("tensor", GP, lw)
                    for es in range(4):
                        for f4 in range(4):
                            wo_pe[emm] = mm(
                                PS[f4][:, 0:512],
                                wob[emm][:, 512 * es + 128 * f4:512 * es + 128 * f4 + 128],
                                at_half[:, 512 * (4 * em + es):512 * (4 * em + es) + 512],
                                em == 0 and es == 0, em == 7 and es == 3,
                                inc=(es == 3 and f4 == 3))
                pe_fg = b.snap(PE)
                b.wait("vector", PE, pe_fg)
                for f4 in range(4):
                    f = 4 * fg + f4
                    ep_prev[f4] = vop(lambda e, f=f, f4=f4: e.tensor_tensor(
                        out=xh_w[:, 512 * f:512 * f + 512], in0=xh_w[:, 512 * f:512 * f + 512],
                        in1=PS[f4][:, 0:512], op=ALU.add))
            pe_gth_gate = b.snap(PE)
            dv_wo_prev = b.snap(DV)
            b.wait("sync", DV, dv_wo_prev)
            st = dma("sync", SY, dst[:, sl].rearrange("(c p) t -> p c t", p=128),
                     xh_w[:].rearrange("p (c t) -> p c t", t=512))
        state["dv_tail"] = dv_wo_prev
        state["prev_store"] = st
        return st

    st1 = ffn(X, x_dram if stage >= 2 else Y, s_in[0], s_out[0])
    state["prev_store"] = st1
    if stage >= 2:
        st2 = attention(x_dram, x_dram if stage >= 3 else Y)
        if stage >= 3:
            ffn(x_dram, Y, s_in[1], s_out[1])

    b.wait("sync", SY, b.snap(SY))
    b.wait("gpsimd", GP, b.snap(GP))
    b.wait("vector", DV, b.snap(DV))
    b.wait("scalar", AC, b.snap(AC))
    b.wait("tensor", PE, b.snap(PE))
    b.check()

    with nc.Block() as block:
        b.run_streams(block)
    return nc



def _host_inputs(inputs):
    x = np.asarray(inputs["L0_attn_x"])[0]
    w_in = np.asarray(inputs["L0_attn_w_in"])
    w_out = np.asarray(inputs["L0_attn_w_out"])
    Wq = np.asarray(inputs["L0_ffn_Wq"]); Wr = np.asarray(inputs["L0_ffn_Wr"])
    ak = np.asarray(inputs["L0_ffn_aux_keys"]); av = np.asarray(inputs["L0_ffn_aux_values"])
    Wvd = np.asarray(inputs["L0_ffn_Wvd"]); Wvu = np.asarray(inputs["L0_ffn_Wvu"])
    Wu1 = np.asarray(inputs["L0_ffn_Wu1"]); bu1 = np.asarray(inputs["L0_ffn_bu1"])
    Wu2 = np.asarray(inputs["L0_ffn_Wu2"]); bu2 = np.asarray(inputs["L0_ffn_bu2"])
    gw1 = np.asarray(inputs["L0_ffn_gate_w1"]); gb = np.asarray(inputs["L0_ffn_gate_bias"])
    logrel = np.asarray(inputs["L0_ffn_log_rel"])

    C = np.ascontiguousarray
    inv_freq = 1.0 / (10000.0 ** (np.arange(0, 128, 2, dtype=np.float64) / 128.0))
    ang = np.arange(S, dtype=np.float64)[:, None] * inv_freq[None, :]
    cosv = np.cos(ang).astype(np.float32); sinv = np.sin(ang).astype(np.float32)
    rope_c = np.concatenate([cosv.T, cosv.T], axis=0).astype(BF16NP)
    rope_s = np.concatenate([-sinv.T, sinv.T], axis=0).astype(BF16NP)
    masksm = np.zeros((128, 2048), dtype=np.float32)
    kk = np.arange(128)[:, None]; qq = np.arange(512)[None, :]
    for j in range(4):
        masksm[:, 512 * j:512 * j + 512] = (128 * j + kk <= qq)
    identt = np.eye(128, dtype=np.float32).astype(BF16NP)
    egm = np.zeros((128, 16), dtype=BF16NP)
    for g in range(4):
        egm[:, 4 * g + g] = 1.0
    sel4m = np.zeros((4, 512), dtype=np.float32)
    for g in range(4):
        sel4m[g, 128 * g:128 * g + 128] = 1.0
    cstsm = np.zeros((128, 8), dtype=np.float32)
    cstsm[:, 0] = 1e-5; cstsm[:, 1] = 1.0; cstsm[:, 2] = -BCONST
    wu2c = C(Wu2.reshape(8, 128).T.astype(np.float32))
    bu1cm = C(bu1.reshape(8, 128).T.astype(np.float32))
    sc3m = np.zeros((1, 8), dtype=np.float32)
    sc3m[0, 0] = float(np.reshape(bu2, -1)[0]); sc3m[0, 1] = float(gw1); sc3m[0, 2] = float(gb)

    common = dict(
        w_o=C(w_out.T), wq_t=C(Wq.T), wr_t=C(Wr.T), ak_t=C(ak.T), av=C(av),
        wvd_t=C(Wvd.T), wvu_t=C(Wvu.T), wu1_t=C(Wu1.T),
        wu2_c=wu2c, bu1_c=bu1cm, sc3=sc3m,
        logrel=C(logrel.reshape(1, 1024).astype(np.float32)),
        rope_c=rope_c, rope_s=rope_s, masks=masksm.astype(BF16NP), identt=identt,
        eg=egm, sel4=sel4m, onesk=np.ones((1, 128), np.float32),
        e8k=np.full((1, 128), 8.0, np.float32),
        ones128=np.ones((128, 1), np.float32), csts=cstsm,
    )

    def qrow(j):
        return 1024 * (j // 8) + 128 * (j % 8)

    in_maps = []
    for c in range(NCORES):
        d = dict(common)
        d["x0"] = C(x[TPC * c:TPC * (c + 1)].T)
        wq_rows = np.concatenate([w_in[qrow(4 * c + g):qrow(4 * c + g) + 128]
                                  for g in range(4)], axis=0)
        d["w_q"] = C(wq_rows.T)
        d["w_k"] = C(w_in[4096 + 128 * c:4096 + 128 * c + 128].T)
        d["w_v"] = C(w_in[5120 + 128 * c:5120 + 128 * c + 128].T)
        idx = np.zeros((128, 64), dtype=np.uint32)
        for ce in range(NCH):
            for hh in range(2):
                e = 128 * ce + np.arange(128)
                idx[:, 2 * ce + hh] = (e * 16 + 2 * c + hh).astype(np.uint32)
        d["idx_wo"] = idx
        in_maps.append(d)
    return in_maps


def kernel(**inputs):
    in_maps = _host_inputs(inputs)
    nc = build_program(_STAGE)
    res = run_bass_kernel_spmd(nc, in_maps, core_ids=list(range(NCORES)))
    out = np.empty((1, S, D), dtype=np.float32)
    for c in range(NCORES):
        out[0, TPC * c:TPC * (c + 1), :] = res.results[c]["y"].T
    return out


# version 03
# speedup vs baseline: 1.5974x; 1.5974x over previous
import numpy as np
import ml_dtypes
import concourse.bass as bass
import concourse.mybir as mybir
from concourse.bass import IndirectOffsetOnAxis
from concourse.bass_utils import run_bass_kernel_spmd

F32 = mybir.dt.float32
BF16 = mybir.dt.bfloat16
U32 = mybir.dt.uint32
AF = mybir.ActivationFunctionType
ALU = mybir.AluOpType
AX = mybir.AxisListType
BF16NP = np.dtype(ml_dtypes.bfloat16)

NCORES = 8
S = 8192
D = 4096
TPC = 1024
NCH = 32
BCONST = 30.0
SC_ATT = 1.0 / float(np.sqrt(128.0))
SC_R = 1.0 / float(np.sqrt(48.0))
GC = 0.7978845608028654
GCA = GC * 0.044715

_STAGE = 3

ENGINES = ("tensor", "vector", "scalar", "gpsimd", "sync")


class Builder:
    def __init__(self):
        self.nc = bass.Bass("TRN2", target_bir_lowering=False)
        self.stream = {e: [] for e in ENGINES}
        self.sems = {}
        self.cnt = {}
        self.items = []
        for s in ("SY", "SC", "GP", "PE", "DV", "AC", "CC"):
            self.sems[s] = self.nc.alloc_semaphore(s)
            self.cnt[s] = 0
        self.SBUF_BASE = 16512
        self.SBUF_MAX = 229344

    def emit(self, eng, fn, inc=None, n=1):
        rec = {"eng": eng, "pos": len(self.stream[eng]), "waits": [], "incs": []}
        if inc is not None:
            self.cnt[inc] += n
            rec["incs"].append((inc, self.cnt[inc]))
            h = self.sems[inc]

            def wrapped(e, fn=fn, h=h, n=n):
                fn(e).then_inc(h, n)

            self.stream[eng].append(wrapped)
        else:
            self.stream[eng].append(fn)
        self.items.append(rec)
        return self.cnt[inc] if inc else None

    def wait(self, eng, sem, val):
        if val <= 0:
            return
        h = self.sems[sem]
        rec = {"eng": eng, "pos": len(self.stream[eng]), "waits": [(sem, val)], "incs": []}
        self.stream[eng].append(lambda e, h=h, v=val: e.wait_ge(h, v))
        self.items.append(rec)

    def snap(self, sem):
        return self.cnt[sem]

    def at(self, name, shape, dtype, off):
        esz = 4 if dtype in (F32, U32) else 2
        nbytes = int(np.prod(shape[1:])) * esz
        assert off >= self.SBUF_BASE and off + nbytes <= self.SBUF_MAX, \
            f"{name}: [{off}, {off}+{nbytes}) outside [{self.SBUF_BASE}, {self.SBUF_MAX})"
        return self.nc.alloc_sbuf_tensor_at(name, list(shape), dtype, offset=off).ap()

    def check(self):
        final = dict(self.cnt)
        producers = {}
        for i, rec in enumerate(self.items):
            for sem, cum in rec["incs"]:
                producers.setdefault(sem, []).append((cum, i))
        n = len(self.items)
        adj = [[] for _ in range(n)]
        laste = {}
        for i, rec in enumerate(self.items):
            e = rec["eng"]
            if e in laste:
                adj[laste[e]].append(i)
            laste[e] = i
            for sem, v in rec["waits"]:
                assert v <= final[sem], f"wait {sem}>={v} but final={final[sem]}"
                src = None
                for cum, j in producers.get(sem, []):
                    if cum >= v:
                        src = j
                        break
                assert src is not None, (sem, v)
                adj[src].append(i)
        state = [0] * n
        order_pos = {}
        stack = []
        for s0 in range(n):
            if state[s0]:
                continue
            stack.append((s0, iter(adj[s0])))
            state[s0] = 1
            while stack:
                node, it = stack[-1]
                advanced = False
                for nxt in it:
                    if state[nxt] == 1:
                        raise RuntimeError(
                            f"sync cycle: item {nxt} eng={self.items[nxt]['eng']} "
                            f"pos={self.items[nxt]['pos']} waits={self.items[nxt]['waits']}")
                    if state[nxt] == 0:
                        state[nxt] = 1
                        stack.append((nxt, iter(adj[nxt])))
                        advanced = True
                        break
                if not advanced:
                    state[node] = 2
                    stack.pop()

    def run_streams(self, block):
        for eng in ENGINES:
            getattr(block, eng)(lambda e, eng=eng: [f(e) for f in self.stream[eng]])


def build_program(stage):
    b = Builder()
    nc = b.nc
    SY, SC, GP, PE, DV, AC, CC = "SY", "SC", "GP", "PE", "DV", "AC", "CC"

    dram = lambda n, s, d, k="ExternalInput": nc.dram_tensor(n, list(s), d, kind=k).ap()
    X = dram("x0", [D, TPC], F32)
    WQ = dram("w_q", [D, 512], F32)
    WK = dram("w_k", [D, 128], F32)
    WV = dram("w_v", [D, 128], F32)
    WO = dram("w_o", [D, D], F32)
    WQT = dram("wq_t", [D, 64], F32)
    WRT = dram("wr_t", [64, 48], F32)
    AKT = dram("ak_t", [64, 1024], F32)
    AV = dram("av", [1024, D], F32)
    WVDT = dram("wvd_t", [D, 256], F32)
    WVUT = dram("wvu_t", [256, D], F32)
    WU1T = dram("wu1_t", [D, 1024], F32)
    WU2C = dram("wu2_c", [128, 8], F32)
    BU1C = dram("bu1_c", [128, 8], F32)
    SC3 = dram("sc3", [1, 8], F32)
    ROPEC = dram("rope_c", [128, S], BF16)
    ROPES = dram("rope_s", [128, S], BF16)
    MASKS = dram("masks", [128, 2048], BF16)
    IDENT = dram("identt", [128, 128], BF16)
    EG = dram("eg", [128, 16], BF16)
    SEL4 = dram("sel4", [4, 512], F32)
    ONESK = dram("onesk", [1, 128], F32)
    ONESKB = dram("oneskb", [1, 128], BF16)
    E8KB = dram("e8kb", [1, 128], BF16)
    LOGRELB = dram("logrelb", [1, 1024], BF16)
    ONES128 = dram("ones128", [128, 1], F32)
    CSTS = dram("csts", [128, 8], F32)
    IDXWO = dram("idx_wo", [128, 64], U32)

    Y = dram("y", [D, TPC], F32, "ExternalOutput")

    x_dram = dram("x_scr", [D, TPC], F32, "Internal")
    xh_in = [dram(f"xh_in{i}", [D, 512], BF16, "Internal") for i in range(2)]
    xh_all = [dram(f"xh_all{i}", [NCORES * D, 512], BF16, "Internal") for i in range(2)]
    q_scr = dram("q_scr", [512, S], BF16, "Internal")
    at_in = dram("at_in", [512, S], BF16, "Internal")
    at_all = dram("at_all", [D, S], BF16, "Internal")
    s_in = [dram(f"s{i}_in", [1, 1], F32, "Internal") for i in range(2)]
    s_out = [dram(f"s{i}_out", [8, 1], F32, "Internal") for i in range(2)]

    o = [b.SBUF_BASE]
    def P(name, shape, dtype):
        ap = b.at(name, shape, dtype, o[0])
        esz = 4 if dtype in (F32, U32) else 2
        o[0] += (int(np.prod(shape[1:])) * esz + 31) // 32 * 32
        return ap

    m_row = P("m_row", [1, 1024], F32)
    rstd_row = P("rstd_row", [1, 1024], F32)
    logv_row = P("logv_row", [1, 1024], F32)
    rt1 = P("rt1", [1, 512], F32)
    rt2 = P("rt2", [1, 512], F32)
    rt3 = P("rt3", [1, 512], F32)
    gate_row = P("gate_row", [1, 1024], F32)
    c2_row = P("c2_row", [1, 1024], F32)
    gsc = P("gsc", [1, 16], F32)
    sc3 = P("sc3_sb", [1, 8], F32)
    csts = P("csts_sb", [128, 8], F32)
    onesk = P("onesk_sb", [1, 128], F32)
    oneskb = P("oneskb_sb", [1, 128], BF16)
    e8kb = P("e8kb_sb", [1, 128], BF16)
    logrelb = P("logrelb_sb", [1, 1024], BF16)
    c2bf = P("c2bf", [1, 512], BF16)
    ones128 = P("ones128_sb", [128, 1], F32)
    idxwo = P("idxwo_sb", [128, 64], U32)
    top8 = P("top8", [128, 8], F32)
    colA = P("colA", [128, 1], F32)
    colB = P("colB", [128, 1], F32)
    colC = P("colC", [128, 1], F32)
    colD = P("colD", [128, 1], F32)
    wu2c = P("wu2c_sb", [128, 8], BF16)
    bu1c = P("bu1c_sb", [128, 8], F32)
    eg = P("eg_sb", [128, 16], BF16)
    ident = P("ident_sb", [128, 128], BF16)
    sel4 = P("sel4_sb", [4, 512], F32)
    masks = P("masks_sb", [128, 2048], BF16)
    ak_bf = P("ak_bf", [64, 1024], BF16)
    rk_bf = P("rk_bf", [48, 1024], BF16)
    wr_bf = P("wr_bf", [64, 48], BF16)
    wq_bf = P("wq_bf", [128, 2048], BF16)
    WBASE = (o[0] + 63) // 64 * 64

    off = WBASE
    def nxt(nbytes):
        nonlocal off
        r = off
        off += (nbytes + 31) // 32 * 32
        return r
    O_XH = nxt(32 * 512 * 4)
    O_H = nxt(32 * 512 * 2)
    O_SQ0 = nxt(512 * 4)
    O_SQ1 = nxt(512 * 4)
    O_WU1 = [nxt(1024 * 2) for _ in range(4)]
    O_U1 = nxt(8 * 512 * 2)
    O_RT = nxt(1024 * 4)
    O_TT = nxt(1024 * 4)
    O_WTK = nxt(1024 * 4)
    O_WBF = nxt(1024 * 2)
    O_GX = nxt(512 * 4)
    O_G1 = nxt(512 * 4)
    O_G2 = nxt(512 * 4)
    O_BCA = nxt(512 * 4)
    O_BCB = nxt(512 * 4)
    O_AVB = nxt(4 * 512 * 2)
    O_WVDB = nxt(2 * 4 * 256 * 2)
    O_A1 = nxt(4 * 512 * 2)
    O_A2 = nxt(2 * 512 * 2)
    assert off <= b.SBUF_MAX, off

    xh = b.at("xh", [128, 32 * 512], F32, O_XH)
    h_sb = b.at("h_sb", [128, 32 * 512], BF16, O_H)
    sqb = [b.at("sq0", [128, 512], F32, O_SQ0), b.at("sq1", [128, 512], F32, O_SQ1)]
    wu1b = [b.at(f"wu1b{i}", [128, 1024], BF16, O_WU1[i]) for i in range(4)]
    u1sb = b.at("u1sb", [128, 8 * 512], BF16, O_U1)
    Rt = b.at("Rt", [128, 1024], F32, O_RT)
    Tt = b.at("Tt", [128, 1024], F32, O_TT)
    wtok = b.at("wtok", [128, 1024], F32, O_WTK)
    wbf = b.at("wbf", [128, 1024], BF16, O_WBF)
    gx = b.at("gx", [128, 512], F32, O_GX)
    g1 = b.at("g1", [128, 512], F32, O_G1)
    g2 = b.at("g2", [128, 512], F32, O_G2)
    bcA = b.at("bcA", [128, 512], F32, O_BCA)
    bcB = b.at("bcB", [128, 512], F32, O_BCB)
    avb = [b.at(f"avb{i}", [128, 512], BF16, O_AVB + 1024 * i) for i in range(4)]
    wvdb = [b.at("wvdb0", [128, 4 * 256], BF16, O_WVDB),
            b.at("wvdb1", [128, 4 * 256], BF16, O_WVDB + 2048)]
    a1sb = b.at("a1sb", [128, 4 * 512], BF16, O_A1)
    a2sb = b.at("a2sb", [128, 2 * 512], BF16, O_A2)
    qryT = b.at("qryT", [64, 512], BF16, O_G1)
    rqT = b.at("rqT", [48, 512], BF16, O_G2)
    wT = b.at("wT", [128, 8 * 512], BF16, O_SQ0)
    wvub = b.at("wvub", [128, 2 * 4096], BF16, O_U1)

    import contextlib
    pctx = contextlib.ExitStack()
    PSALL = pctx.enter_context(nc.psum_tensor("psall", [128, 4096], F32))
    PS = [PSALL[:, 512 * i:512 * i + 512] for i in range(8)]

    def dma(qeng, qsem, out, in_):
        return b.emit(qeng, lambda e, out=out, in_=in_: e.dma_start(out=out, in_=in_),
                      inc=qsem, n=16)

    def mm(out, lhsT, rhs, start, stop, inc=False):
        return b.emit("tensor",
                      lambda e, out=out, l=lhsT, r=rhs, s1=start, s2=stop:
                      e.matmul(out, l, r, start=s1, stop=s2),
                      inc=(PE if inc else None)) or b.snap(PE)

    def vop(fn, inc=True):
        return b.emit("vector", fn, inc=(DV if inc else None)) or b.snap(DV)

    def aop(fn, inc=True):
        return b.emit("scalar", fn, inc=(AC if inc else None)) or b.snap(AC)


    dma("sync", SY, csts, CSTS)
    dma("sync", SY, sc3, SC3)
    dma("sync", SY, onesk, ONESK)
    dma("sync", SY, oneskb, ONESKB)
    dma("sync", SY, e8kb, E8KB)
    dma("sync", SY, logrelb, LOGRELB)
    dma("sync", SY, ones128, ONES128)
    dma("sync", SY, idxwo, IDXWO)
    cast = lambda out, in_: dma("gpsimd", "GP", out, in_)
    cast(wu2c, WU2C)
    dma("sync", SY, bu1c, BU1C)
    dma("sync", SY, eg, EG)
    dma("sync", SY, ident, IDENT)
    dma("sync", SY, sel4, SEL4)
    consts_sy = dma("sync", SY, masks, MASKS)
    cast(ak_bf, AKT)
    cast(wr_bf, WRT)
    consts_gp = cast(wq_bf[:].rearrange("p (c n) -> p c n", n=64),
                     WQT.rearrange("(c p) n -> p c n", p=128))
    b.wait("tensor", "GP", consts_gp)

    b.wait("vector", GP, consts_gp)

    mm(PS[0][0:48, 0:512], wr_bf[:], ak_bf[:, 0:512], True, True)
    pe_rk = mm(PS[1][0:48, 0:512], wr_bf[:], ak_bf[:, 512:1024], True, True, inc=True)
    b.wait("vector", PE, pe_rk)
    vop(lambda e: e.tensor_scalar_mul(rk_bf[:, 0:512], PS[0][0:48, 0:512], SC_R), inc=False)
    dv_rk = vop(lambda e: e.tensor_scalar_mul(rk_bf[:, 512:1024], PS[1][0:48, 0:512], SC_R))

    state = {"prev_store": 0, "dv_rk": dv_rk, "dv_tail": 0}

    def ffn(src, dst, sin_d, sout_d):
        b.wait("tensor", DV, state["dv_tail"])
        b.wait("sync", SY, state["prev_store"])
        stat_pe = {}
        sq_pe = {0: 0, 1: 0}
        ac_sq = 0
        for h in (0, 1):
            if h == 1:
                b.wait("sync", PE, stat_pe[0])
                b.wait("sync", AC, ac_sq)
            ld = dma("sync", SY, xh[:].rearrange("p (c t) -> p c t", t=512),
                     src[:, 512 * h:512 * h + 512].rearrange("(c p) t -> p c t", p=128))
            b.wait("scalar", SY, ld)
            b.wait("tensor", SY, ld)
            if h == 0:
                b.wait("tensor", DV, state["dv_rk"])
            for c in range(NCH):
                if c >= 2 or h == 1:
                    b.wait("scalar", PE, sq_pe[c % 2])
                ac_sq = aop(lambda e, c=c: e.activation(
                    out=sqb[c % 2][:], in_=xh[:, 512 * c:512 * c + 512], func=AF.Square))
                b.wait("tensor", AC, ac_sq)
                mm(PS[0 + h][0:1, 0:512], ones128[:], xh[:, 512 * c:512 * c + 512],
                   c == 0, c == NCH - 1)
                sq_pe[c % 2] = mm(PS[2 + h][0:1, 0:512], ones128[:], sqb[c % 2][:],
                                  c == 0, c == NCH - 1, inc=True)
            stat_pe[h] = b.snap(PE)
        b.wait("vector", PE, stat_pe[1])
        ac_logv = 0
        for h in (0, 1):
            sl = slice(512 * h, 512 * h + 512)
            if h == 1:
                b.wait("vector", AC, ac_logv)
            vop(lambda e, sl=sl: e.tensor_scalar_mul(m_row[0:1, sl], PS[0 + h][0:1, 0:512],
                                                     1.0 / D), inc=False)
            dv_m = vop(lambda e: e.engine_nop())
            b.wait("vector", DV, dv_m)
            vop(lambda e, sl=sl: e.tensor_tensor(out=rt1[0:1, 0:512], in0=m_row[0:1, sl],
                                                 in1=m_row[0:1, sl], op=ALU.mult), inc=False)
            dv_v = vop(lambda e, h=h: e.scalar_tensor_tensor(
                out=rt2[0:1, 0:512], in0=PS[2 + h][0:1, 0:512], scalar=1.0 / D,
                in1=rt1[0:1, 0:512], op0=ALU.mult, op1=ALU.subtract))
            b.wait("scalar", DV, dv_v)
            ac_ln = aop(lambda e: e.activation(out=rt3[0:1, 0:512], in_=rt2[0:1, 0:512],
                                               func=AF.Ln, bias=csts[0:1, 0:1]))
            b.wait("scalar", AC, ac_ln)
            ac_rstd = aop(lambda e, sl=sl: e.activation(out=rstd_row[0:1, sl],
                                                        in_=rt3[0:1, 0:512],
                                                        func=AF.Exp, scale=-0.5))
            b.wait("vector", AC, ac_ln)
            vop(lambda e: e.tensor_scalar_add(rt1[0:1, 0:512], rt2[0:1, 0:512], 1e-5),
                inc=False)
            dv_a = vop(lambda e: e.engine_nop())
            b.wait("vector", DV, dv_a)
            dv_b = vop(lambda e: e.reciprocal(rt1[0:1, 0:512], rt1[0:1, 0:512]))
            b.wait("vector", DV, dv_b)
            dv_vh = vop(lambda e: e.tensor_tensor(out=rt1[0:1, 0:512], in0=rt2[0:1, 0:512],
                                                  in1=rt1[0:1, 0:512], op=ALU.mult))
            b.wait("scalar", DV, dv_vh)
            ac_logv = aop(lambda e, sl=sl: e.activation(out=logv_row[0:1, sl],
                                                        in_=rt1[0:1, 0:512], func=AF.Ln,
                                                        bias=csts[0:1, 1:2]))
        b.wait("vector", AC, ac_logv)
        dv_part = vop(lambda e: e.reduce_sum(out=gsc[0:1, 0:1], in_=logv_row[:], axis=AX.X))
        b.wait("gpsimd", DV, dv_part)
        gp_s = dma("gpsimd", GP, sin_d, gsc[0:1, 0:1])
        b.wait("gpsimd", GP, gp_s)
        cc_done = b.emit("gpsimd", lambda e: e.collective_compute(
            "AllGather", ALU.bypass, ins=[sin_d.opt()], outs=[sout_d.opt()],
            replica_groups=[list(range(NCORES))]), inc=CC, n=1)
        b.wait("gpsimd", CC, cc_done)
        gp_ag = dma("gpsimd", GP, gsc[0:1, 4:12], sout_d.rearrange("a b -> b a"))
        b.wait("vector", GP, gp_ag)
        vop(lambda e: e.reduce_sum(out=gsc[0:1, 1:2], in_=gsc[0:1, 4:12], axis=AX.X), inc=False)
        dv_tot = vop(lambda e: e.engine_nop())
        b.wait("vector", DV, dv_tot)
        vop(lambda e: e.tensor_scalar(out=gsc[0:1, 2:3], in0=gsc[0:1, 1:2], scalar1=1.0 / S,
                                      scalar2=1e-6, op0=ALU.mult, op1=ALU.add), inc=False)
        dv_t2 = vop(lambda e: e.engine_nop())
        b.wait("vector", DV, dv_t2)
        dv_invm = vop(lambda e: e.reciprocal(gsc[0:1, 2:3], gsc[0:1, 2:3]))

        wu1_pe = {0: 0, 1: 0, 2: 0, 3: 0}
        av_pe = {0: 0, 1: 0, 2: 0, 3: 0}
        wvd_pe = {0: 0, 1: 0}
        st = 0
        for hi, h in enumerate((1, 0)):
            sl = slice(512 * h, 512 * h + 512)
            if hi == 1:
                b.wait("tensor", DV, dv_ep_prev)
                b.wait("sync", SY, st)
                ld = dma("sync", SY, xh[:].rearrange("p (c t) -> p c t", t=512),
                         src[:, sl].rearrange("(c p) t -> p c t", p=128))
                b.wait("vector", SY, ld)
                b.wait("tensor", SY, ld)
            b.wait("tensor", AC, ac_rstd)
            mm(PS[5][:, 0:512], onesk[:], m_row[0:1, sl], True, True)
            pe_bc = mm(PS[6][:, 0:512], onesk[:], rstd_row[0:1, sl], True, True, inc=True)
            b.wait("vector", PE, pe_bc)
            vop(lambda e: e.tensor_copy(out=bcA[:], in_=PS[5][:, 0:512]), inc=False)
            dv_bc = vop(lambda e: e.tensor_copy(out=bcB[:], in_=PS[6][:, 0:512]))
            for c in range(NCH):
                vop(lambda e, c=c: e.tensor_tensor(out=gx[:], in0=xh[:, 512 * c:512 * c + 512],
                                                   in1=bcA[:], op=ALU.subtract), inc=False)
                vop(lambda e, c=c: e.tensor_tensor(out=h_sb[:, 512 * c:512 * c + 512],
                                                   in0=gx[:], in1=bcB[:], op=ALU.mult),
                    inc=(c == NCH - 1))
            dv_h = b.snap(DV)
            b.wait("tensor", DV, dv_h)
            for c in range(NCH):
                cm = c % 4
                b.wait("gpsimd", PE, wu1_pe[cm])
                ldw = cast(wu1b[cm][:], WU1T[128 * c:128 * c + 128, :])
                b.wait("tensor", GP, ldw)
                for u in range(8):
                    wu1_pe[cm] = mm(PS[u][:, 0:512], wu1b[cm][:, 128 * u:128 * u + 128],
                                    h_sb[:, 512 * c:512 * c + 512], c == 0, c == NCH - 1,
                                    inc=(u == 7))
            pe_u1 = b.snap(PE)
            b.wait("vector", PE, pe_u1)
            ac2 = 0
            for u in range(8):
                b.wait("vector", AC, ac2)
                dv0 = vop(lambda e, u=u: e.tensor_scalar(
                    out=gx[:], in0=PS[u][:, 0:512], scalar1=bu1c[:, u:u + 1],
                    scalar2=None, op0=ALU.add))
                b.wait("scalar", DV, dv0)
                ac1 = aop(lambda e: e.activation(out=g1[:], in_=gx[:], func=AF.Square))
                b.wait("vector", AC, ac1)
                vop(lambda e: e.tensor_scalar(out=g1[:], in0=g1[:], scalar1=GCA, scalar2=GC,
                                              op0=ALU.mult, op1=ALU.add), inc=False)
                dv1 = vop(lambda e: e.tensor_tensor(out=g1[:], in0=g1[:], in1=gx[:], op=ALU.mult))
                b.wait("scalar", DV, dv1)
                ac2 = aop(lambda e: e.activation(out=g2[:], in_=g1[:], func=AF.Tanh))
                b.wait("vector", AC, ac2)
                vop(lambda e: e.tensor_scalar(out=g2[:], in0=g2[:], scalar1=0.5, scalar2=0.5,
                                              op0=ALU.mult, op1=ALU.add), inc=False)
                dv_prev = vop(lambda e, u=u: e.tensor_tensor(
                    out=u1sb[:, 512 * u:512 * u + 512], in0=g2[:], in1=gx[:], op=ALU.mult))
            dv_gelu = dv_prev
            b.wait("tensor", DV, dv_gelu)
            for u in range(8):
                mm(PS[0][0:1, 0:512], wu2c[:, u:u + 1], u1sb[:, 512 * u:512 * u + 512],
                   u == 0, u == 7, inc=(u == 7))
            pe_lr = b.snap(PE)
            b.wait("vector", PE, pe_lr)
            vop(lambda e: e.tensor_scalar(out=rt1[0:1, 0:512], in0=PS[0][0:1, 0:512],
                                          scalar1=sc3[0:1, 0:1], scalar2=None, op0=ALU.add),
                inc=False)
            dv_l = vop(lambda e: e.engine_nop())
            b.wait("scalar", DV, dv_l)
            ac_sg = aop(lambda e: e.activation(out=rt2[0:1, 0:512], in_=rt1[0:1, 0:512],
                                               func=AF.Tanh, scale=0.5))
            b.wait("vector", AC, ac_sg)
            b.wait("vector", DV, dv_invm)
            vop(lambda e: e.tensor_scalar(out=rt2[0:1, 0:512], in0=rt2[0:1, 0:512], scalar1=1.25,
                                          scalar2=1.25, op0=ALU.mult, op1=ALU.add), inc=False)
            vop(lambda e, sl=sl: e.tensor_scalar(out=rt3[0:1, 0:512], in0=logv_row[0:1, sl],
                                                 scalar1=gsc[0:1, 2:3], scalar2=0.5,
                                                 op0=ALU.mult, op1=ALU.mult), inc=False)
            vop(lambda e: e.tensor_tensor(out=rt2[0:1, 0:512], in0=rt2[0:1, 0:512],
                                          in1=rt3[0:1, 0:512], op=ALU.add), inc=False)
            vop(lambda e: e.tensor_scalar(out=rt2[0:1, 0:512], in0=rt2[0:1, 0:512], scalar1=0.0,
                                          scalar2=5.0, op0=ALU.max, op1=ALU.min), inc=False)
            vop(lambda e: e.tensor_scalar(out=rt2[0:1, 0:512], in0=rt2[0:1, 0:512], scalar1=0.5,
                                          scalar2=1.0 / 1.5, op0=ALU.subtract, op1=ALU.mult),
                inc=False)
            vop(lambda e: e.tensor_scalar(out=rt2[0:1, 0:512], in0=rt2[0:1, 0:512], scalar1=0.0,
                                          scalar2=1.0, op0=ALU.max, op1=ALU.min), inc=False)
            vop(lambda e: e.tensor_scalar(out=rt2[0:1, 0:512], in0=rt2[0:1, 0:512],
                                          scalar1=sc3[0:1, 1:2], scalar2=sc3[0:1, 2:3],
                                          op0=ALU.mult, op1=ALU.add), inc=False)
            dv_z = vop(lambda e: e.engine_nop())
            b.wait("scalar", DV, dv_z)
            ac_g = aop(lambda e: e.activation(out=rt3[0:1, 0:512], in_=rt2[0:1, 0:512],
                                              func=AF.Tanh, scale=0.5))
            b.wait("vector", AC, ac_g)
            vop(lambda e: e.tensor_scalar(out=rt3[0:1, 0:512], in0=rt3[0:1, 0:512], scalar1=0.5,
                                          scalar2=0.5, op0=ALU.mult, op1=ALU.add), inc=False)
            dv_g1 = vop(lambda e: e.engine_nop())
            b.wait("vector", DV, dv_g1)
            vop(lambda e: e.tensor_scalar(out=rt2[0:1, 0:512], in0=rt3[0:1, 0:512], scalar1=0.05,
                                          scalar2=None, op0=ALU.is_ge), inc=False)
            vop(lambda e: e.tensor_tensor(out=rt3[0:1, 0:512], in0=rt3[0:1, 0:512],
                                          in1=rt2[0:1, 0:512], op=ALU.mult), inc=False)
            dv_gate = vop(lambda e, sl=sl: e.tensor_scalar_mul(gate_row[0:1, sl],
                                                               rt3[0:1, 0:512], 0.5))

            for c in range(NCH):
                mm(PS[7][0:64, 0:512], wq_bf[:, 64 * c:64 * c + 64],
                   h_sb[:, 512 * c:512 * c + 512], c == 0, c == NCH - 1, inc=(c == NCH - 1))
            pe_q = b.snap(PE)
            b.wait("vector", PE, pe_q)
            dv_q = vop(lambda e: e.tensor_copy(out=qryT[:], in_=PS[7][0:64, 0:512]))
            b.wait("tensor", DV, dv_q)
            pe_rq = mm(PS[6][0:48, 0:512], wr_bf[:], qryT[:], True, True, inc=True)
            b.wait("vector", PE, pe_rq)
            dv_rq = vop(lambda e: e.tensor_scalar_mul(rqT[:], PS[6][0:48, 0:512], SC_R))
            b.wait("tensor", DV, dv_rq)
            b.wait("tensor", DV, dv_gate)

            dv_cd = 0
            pe_tr_last = 0
            for t4 in range(4):
                ts0 = 128 * t4
                if t4 > 0:
                    b.wait("tensor", DV, dv_cd)
                mm(PS[0][:, 0:512], rqT[:, ts0:ts0 + 128], rk_bf[:, 0:512], True, False)
                mm(PS[0][:, 0:512], oneskb[:], logrelb[0:1, 0:512], False, True)
                mm(PS[1][:, 0:512], rqT[:, ts0:ts0 + 128], rk_bf[:, 512:1024], True, False)
                mm(PS[1][:, 0:512], oneskb[:], logrelb[0:1, 512:1024], False, True)
                mm(PS[2][:, 0:512], qryT[:, ts0:ts0 + 128], ak_bf[:, 0:512], True, False)
                mm(PS[2][:, 0:512], e8kb[:], logrelb[0:1, 0:512], False, True)
                mm(PS[3][:, 0:512], qryT[:, ts0:ts0 + 128], ak_bf[:, 512:1024], True, False)
                mm(PS[3][:, 0:512], e8kb[:], logrelb[0:1, 512:1024], False, True)
                pe_r = mm(PS[4][:, 0:1], gate_row[0:1, 512 * h + ts0:512 * h + ts0 + 128],
                          onesk[0:1, 0:1], True, True, inc=True)
                b.wait("vector", PE, pe_r)
                dv_m8 = vop(lambda e: e.max(out=top8[:], in_=PSALL[:, 0:1024]))
                b.wait("vector", DV, dv_m8)
                dv_R = vop(lambda e: e.tensor_scalar(out=wtok[:], in0=PSALL[:, 0:1024],
                                                     scalar1=top8[:, 7:8],
                                                     scalar2=None, op0=ALU.is_lt))
                vop(lambda e: e.scalar_tensor_tensor(out=Tt[:, 0:512], in0=wtok[:, 0:512],
                                                     scalar=-1e30, in1=PS[2][:, 0:512],
                                                     op0=ALU.mult, op1=ALU.add), inc=False)
                dv_T = vop(lambda e: e.scalar_tensor_tensor(
                    out=Tt[:, 512:1024], in0=wtok[:, 512:1024], scalar=-1e30,
                    in1=PS[3][:, 0:512], op0=ALU.mult, op1=ALU.add))
                b.wait("vector", DV, dv_T)
                dv_mx = vop(lambda e: e.reduce_max(out=colA[:], in_=Tt[:], axis=AX.X))
                b.wait("vector", DV, dv_mx)
                dv_nb = vop(lambda e: e.tensor_scalar_mul(colB[:], colA[:], -0.125))
                b.wait("scalar", DV, dv_nb)
                ac_e = aop(lambda e: e.activation(out=wtok[:], in_=Tt[:], func=AF.Exp,
                                                  bias=colB[:], scale=0.125))
                b.wait("vector", AC, ac_e)
                dv_s8 = vop(lambda e: e.reduce_sum(out=colC[:], in_=wtok[:], axis=AX.X))
                b.wait("vector", DV, dv_s8)
                dv_r8 = vop(lambda e: e.reciprocal(colC[:], colC[:]))
                b.wait("vector", DV, dv_r8)
                dv_cd = vop(lambda e: e.tensor_tensor(out=colD[:], in0=PS[4][:, 0:1],
                                                      in1=colC[:], op=ALU.mult))
                b.wait("vector", DV, dv_cd)
                b.wait("vector", PE, pe_tr_last)
                dv_wb = vop(lambda e: e.tensor_scalar(out=wbf[:], in0=wtok[:], scalar1=colD[:],
                                                      scalar2=None, op0=ALU.mult))
                b.wait("tensor", DV, dv_wb)
                pst = PS[5][:, 0:512].bitcast(BF16)
                dv_tr = 0
                for n in range(8):
                    if n > 0:
                        b.wait("tensor", DV, dv_tr)
                    pe_tr_last = b.emit("tensor",
                                        lambda e, n=n, pst=pst: e.transpose(
                                            pst[:, 0:128], wbf[:, 128 * n:128 * n + 128], ident[:]),
                                        inc=PE)
                    b.wait("vector", PE, pe_tr_last)
                    dv_tr = vop(lambda e, n=n, ts0=ts0, pst=pst: e.tensor_copy(
                        out=wT[:, 512 * n + ts0:512 * n + ts0 + 128], in_=pst[:, 0:128]))
            dv_wT = dv_tr

            b.wait("tensor", DV, dv_wT)
            for g in range(8):
                gm = g % 2
                b.wait("gpsimd", PE, wvd_pe[gm])
                ldw = cast(wvdb[gm][:].rearrange("p (a n) -> p a n", a=4),
                           WVDT[512 * g:512 * g + 512, :].rearrange("(a p) n -> p a n", p=128))
                for n in range(8):
                    nm = n % 4
                    b.wait("gpsimd", PE, av_pe[nm])
                    lda = cast(avb[nm][:], AV[128 * n:128 * n + 128, 512 * g:512 * g + 512])
                    b.wait("tensor", GP, lda)
                    for u4 in range(4):
                        av_pe[nm] = mm(PS[u4][:, 0:512], avb[nm][:, 128 * u4:128 * u4 + 128],
                                       wT[:, 512 * n:512 * n + 512], n == 0, n == 7,
                                       inc=(u4 == 3))
                pe_a1 = b.snap(PE)
                b.wait("vector", PE, pe_a1)
                for u4 in range(4):
                    vop(lambda e, u4=u4: e.tensor_copy(out=a1sb[:, 512 * u4:512 * u4 + 512],
                                                       in_=PS[u4][:, 0:512]),
                        inc=(u4 == 3))
                dv_a1 = b.snap(DV)
                b.wait("tensor", DV, dv_a1)
                b.wait("tensor", GP, ldw)
                for u4 in range(4):
                    for v2 in range(2):
                        wvd_pe[gm] = mm(
                            PS[4 + v2][:, 0:512],
                            wvdb[gm][:, 256 * u4 + 128 * v2:256 * u4 + 128 * v2 + 128],
                            a1sb[:, 512 * u4:512 * u4 + 512],
                            (g == 0 and u4 == 0), (g == 7 and u4 == 3),
                            inc=(u4 == 3 and v2 == 1))
            pe_a2 = b.snap(PE)
            b.wait("vector", PE, pe_a2)
            vop(lambda e: e.tensor_copy(out=a2sb[:, 0:512], in_=PS[4][:, 0:512]), inc=False)
            dv_a2 = vop(lambda e: e.tensor_copy(out=a2sb[:, 512:1024], in_=PS[5][:, 0:512]))
            vop(lambda e, sl=sl: e.tensor_tensor(out=c2_row[0:1, sl], in0=m_row[0:1, sl],
                                                 in1=rstd_row[0:1, sl], op=ALU.mult), inc=False)
            dv_c2a = vop(lambda e: e.engine_nop())
            b.wait("vector", DV, dv_c2a)
            vop(lambda e, sl=sl: e.tensor_scalar_mul(c2_row[0:1, sl], c2_row[0:1, sl], -0.5),
                inc=False)
            vop(lambda e, sl=sl: e.tensor_copy(out=c2bf[:], in_=c2_row[0:1, sl]), inc=False)
            dv_c1 = vop(lambda e, sl=sl: e.tensor_scalar(
                out=rt1[0:1, 0:512], in0=rstd_row[0:1, sl], scalar1=0.5, scalar2=1.0,
                op0=ALU.mult, op1=ALU.add))
            b.wait("tensor", DV, dv_c1)
            pe_c1b = mm(PS[6][:, 0:512], onesk[:], rt1[0:1, 0:512], True, True, inc=True)
            b.wait("vector", PE, pe_c1b)
            dv_bc1 = vop(lambda e: e.tensor_copy(out=bcA[:], in_=PS[6][:, 0:512]))
            ldu = cast(wvub[:].rearrange("p (a n) -> p a n", a=2),
                       WVUT.rearrange("(a p) n -> p a n", p=128))
            b.wait("tensor", GP, ldu)
            b.wait("tensor", DV, dv_a2)
            b.wait("tensor", DV, dv_bc1)
            ep_dv = {6: 0, 7: 0}
            for f in range(NCH):
                fb = 6 + (f % 2)
                b.wait("tensor", DV, ep_dv[fb])
                for v2 in range(2):
                    mm(PS[fb][:, 0:512], wvub[:, 4096 * v2 + 128 * f:4096 * v2 + 128 * f + 128],
                       a2sb[:, 512 * v2:512 * v2 + 512], v2 == 0, False)
                pe_f = mm(PS[fb][:, 0:512], oneskb[:], c2bf[0:1, 0:512], False, True, inc=True)
                b.wait("vector", PE, pe_f)
                vop(lambda e, f=f: e.tensor_tensor(out=gx[:], in0=xh[:, 512 * f:512 * f + 512],
                                                   in1=bcA[:], op=ALU.mult), inc=False)
                ep_dv[fb] = vop(lambda e, f=f, fb=fb: e.tensor_tensor(
                    out=xh[:, 512 * f:512 * f + 512], in0=gx[:], in1=PS[fb][:, 0:512],
                    op=ALU.add))
            dv_ep = b.snap(DV)
            dv_ep_prev = dv_ep
            b.wait("sync", DV, dv_ep)
            st = dma("sync", SY, dst[:, sl].rearrange("(c p) t -> p c t", p=128),
                     xh[:].rearrange("p (c t) -> p c t", t=512))
        state["dv_tail"] = dv_ep_prev
        state["prev_store"] = st
        return st


    def attention(src, dst):
        b.wait("tensor", DV, state["dv_tail"])
        AB = WBASE
        K_sb = b.at("K_sb", [128, S], BF16, AB)
        V_sb = b.at("V_sb", [128, S], BF16, AB + 16384)
        xq = [b.at("xq0", [128, 32 * 512], BF16, AB + 32768),
              b.at("xq1", [128, 32 * 512], BF16, AB + 65536)]
        wqkv = b.at("wqkv", [128, 32 * 768], BF16, AB + 98304)
        rpc = [b.at("rpc0", [128, 512], BF16, AB + 147456),
               b.at("rpc1", [128, 512], BF16, AB + 148480)]
        rps = [b.at("rps0", [128, 512], BF16, AB + 149504),
               b.at("rps1", [128, 512], BF16, AB + 150528)]
        racc = b.at("racc", [128, 512], F32, AB + 151552)
        rtmp = b.at("rtmp", [128, 512], F32, AB + 153600)
        qkf = b.at("qkf", [128, 6 * 512], BF16, AB + 158720)
        qst = [b.at("qst0", [128, 512], BF16, AB + 155648),
               b.at("qst1", [128, 512], BF16, AB + 156672)]
        vtmp = b.at("vtmp", [128, 512], BF16, AB + 157696)
        ML = AB + 32768
        q_tile = [b.at(f"q_tile{i}", [128, 4 * 512], BF16, ML + 4096 * i) for i in range(2)]
        probp = [b.at(f"probp{i}", [128, 1024], BF16, ML + 8192 + 2048 * i) for i in range(2)]
        probm = [b.at(f"probm{i}", [128, 1024], BF16, ML + 12288 + 2048 * i) for i in range(2)]
        attn_t = b.at("attn_t", [128, 4 * 512], BF16, ML + 16384)
        s_sb = b.at("s_sb", [4, 512], F32, ML + 20480)
        rcp = b.at("rcp", [4, 512], F32, ML + 22528)
        bcs = [b.at("bcs0", [128, 512], F32, ML + 24576),
               b.at("bcs1", [128, 512], F32, ML + 26624)]
        xh_w = b.at("xh_w", [128, 32 * 512], F32, AB)
        at_half = b.at("at_half", [128, 32 * 512], BF16, AB + 65536)
        wob = [b.at(f"wob{i}", [128, 4 * 512], BF16, AB + 98304 + 4096 * i) for i in range(4)]

        sq_pe = {0: 0, 1: 0}
        ac_sq = 0
        stat_pe = 0
        xst = 0
        dv_v_prev = 0
        b.wait("sync", SY, state["prev_store"])
        for h in (0, 1):
            if h == 1:
                b.wait("sync", PE, stat_pe)
                b.wait("sync", AC, ac_sq)
                b.wait("sync", DV, dv_hd)
            ld = dma("sync", SY, xh[:].rearrange("p (c t) -> p c t", t=512),
                     src[:, 512 * h:512 * h + 512].rearrange("(c p) t -> p c t", p=128))
            b.wait("scalar", SY, ld)
            b.wait("tensor", SY, ld)
            b.wait("tensor", DV, dv_v_prev if h == 1 else state["dv_tail"])
            for c in range(NCH):
                if c >= 2 or h == 1:
                    b.wait("scalar", PE, sq_pe[c % 2])
                ac_sq = aop(lambda e, c=c: e.activation(
                    out=sqb[c % 2][:], in_=xh[:, 512 * c:512 * c + 512], func=AF.Square))
                b.wait("tensor", AC, ac_sq)
                mm(PS[0][0:1, 0:512], ones128[:], xh[:, 512 * c:512 * c + 512],
                   c == 0, c == NCH - 1)
                sq_pe[c % 2] = mm(PS[2][0:1, 0:512], ones128[:], sqb[c % 2][:],
                                  c == 0, c == NCH - 1, inc=True)
            stat_pe = b.snap(PE)
            sl = slice(512 * h, 512 * h + 512)
            b.wait("vector", PE, stat_pe)
            vop(lambda e, sl=sl: e.tensor_scalar_mul(m_row[0:1, sl], PS[0][0:1, 0:512],
                                                     1.0 / D), inc=False)
            dv_m = vop(lambda e: e.engine_nop())
            b.wait("vector", DV, dv_m)
            vop(lambda e, sl=sl: e.tensor_tensor(out=rt1[0:1, 0:512], in0=m_row[0:1, sl],
                                                 in1=m_row[0:1, sl], op=ALU.mult), inc=False)
            dv_v = vop(lambda e: e.scalar_tensor_tensor(
                out=rt2[0:1, 0:512], in0=PS[2][0:1, 0:512], scalar=1.0 / D,
                in1=rt1[0:1, 0:512], op0=ALU.mult, op1=ALU.subtract))
            dv_v_prev = dv_v
            b.wait("scalar", DV, dv_v)
            ac_ln = aop(lambda e: e.activation(out=rt3[0:1, 0:512], in_=rt2[0:1, 0:512],
                                               func=AF.Ln, bias=csts[0:1, 0:1]))
            b.wait("scalar", AC, ac_ln)
            ac_rstd = aop(lambda e, sl=sl: e.activation(out=rstd_row[0:1, sl],
                                                        in_=rt3[0:1, 0:512],
                                                        func=AF.Exp, scale=-0.5))
            b.wait("tensor", AC, ac_rstd)
            b.wait("tensor", DV, dv_v)
            mm(PS[5][:, 0:512], onesk[:], m_row[0:1, sl], True, True)
            pe_bc = mm(PS[6][:, 0:512], onesk[:], rstd_row[0:1, sl], True, True, inc=True)
            b.wait("vector", PE, pe_bc)
            vop(lambda e: e.tensor_copy(out=bcA[:], in_=PS[5][:, 0:512]), inc=False)
            vop(lambda e: e.tensor_copy(out=bcB[:], in_=PS[6][:, 0:512]), inc=False)
            for c in range(NCH):
                vop(lambda e, c=c: e.tensor_tensor(out=gx[:], in0=xh[:, 512 * c:512 * c + 512],
                                                   in1=bcA[:], op=ALU.subtract), inc=False)
                vop(lambda e, c=c: e.tensor_tensor(out=h_sb[:, 512 * c:512 * c + 512],
                                                   in0=gx[:], in1=bcB[:], op=ALU.mult),
                    inc=(c == NCH - 1))
            dv_hd = b.snap(DV)
            b.wait("sync", DV, dv_hd)
            xst = dma("sync", SY, xh_in[h].rearrange("(c p) t -> p c t", p=128),
                      h_sb[:].rearrange("p (c t) -> p c t", t=512))
            b.wait("gpsimd", SY, xst)
            cc2 = b.emit("gpsimd", lambda e, h=h: e.collective_compute(
                "AllGather", ALU.bypass, ins=[xh_in[h].opt()],
                outs=[xh_all[h].opt()],
                replica_groups=[list(range(NCORES))]), inc=CC, n=1)

        wq_ld = cast(wqkv[:].rearrange("p (c f) -> p c f", f=768)[:, :, 0:512],
                     WQ.rearrange("(c p) f -> p c f", p=128))
        wk_ld = cast(wqkv[:].rearrange("p (c f) -> p c f", f=768)[:, :, 512:640],
                     WK.rearrange("(c p) f -> p c f", p=128))
        wv_ld = cast(wqkv[:].rearrange("p (c f) -> p c f", f=768)[:, :, 640:768],
                     WV.rearrange("(c p) f -> p c f", p=128))
        b.wait("tensor", GP, wv_ld)
        b.wait("sync", CC, cc2)
        xq_pe = {0: 0, 1: 0}
        rp_dv = {0: 0, 1: 0}
        qst_sy = {0: 0, 1: 0}
        dv_q_tail = 0
        for tb in range(16):
            tm = tb % 2
            rank, lt = tb // 2, (tb % 2) * 512
            b.wait("sync", PE, xq_pe[tm])
            lx = dma("sync", SY, xq[tm][:].rearrange("p (c t) -> p c t", t=512),
                     xh_all[tb % 2][D * rank:D * rank + D, :]
                     .rearrange("(c p) t -> p c t", p=128))
            b.wait("sync", DV, rp_dv[tm])
            lrc = dma("sync", SY, rpc[tm][:], ROPEC[:, 512 * tb:512 * tb + 512])
            lrs = dma("sync", SY, rps[tm][:], ROPES[:, 512 * tb:512 * tb + 512])
            b.wait("tensor", SY, lrs)
            if isinstance(dv_q_tail, tuple):
                b.wait("tensor", DV, dv_q_tail[0])
                b.wait("tensor", AC, dv_q_tail[1])
            for c in range(NCH):
                for f in range(6):
                    xq_pe[tm] = mm(PS[f][:, 0:512],
                                   wqkv[:, 768 * c + 128 * f:768 * c + 128 * f + 128],
                                   xq[tm][:, 512 * c:512 * c + 512],
                                   c == 0, c == NCH - 1, inc=(f == 5))
            pe_qkv = b.snap(PE)
            b.wait("vector", PE, pe_qkv)
            b.wait("scalar", PE, pe_qkv)
            for fi in range(3):
                vop(lambda e, fi=fi: e.tensor_copy(out=qkf[:, 512 * fi:512 * fi + 512],
                                                   in_=PS[fi][:, 0:512]),
                    inc=(fi == 2))
            for fi in range(3, 6):
                aop(lambda e, fi=fi: e.activation(out=qkf[:, 512 * fi:512 * fi + 512],
                                                  in_=PS[fi][:, 0:512], func=AF.Identity),
                    inc=(fi == 5))
            dv_ev = b.snap(DV)
            ac_ev = b.snap(AC)
            dv_q_tail = (dv_ev, ac_ev)
            b.wait("vector", AC, ac_ev)
            for fi in range(5):
                dst_is_k = (fi == 4)
                qm = fi % 2
                if not dst_is_k:
                    b.wait("vector", SY, qst_sy[qm])
                vop(lambda e, fi=fi, tm=tm: e.tensor_tensor(
                    out=racc[:], in0=qkf[:, 512 * fi:512 * fi + 512], in1=rpc[tm][:],
                    op=ALU.mult), inc=False)
                vop(lambda e, fi=fi, tm=tm: e.tensor_tensor(
                    out=rtmp[0:64, :], in0=qkf[64:128, 512 * fi:512 * fi + 512],
                    in1=rps[tm][64:128, :], op=ALU.mult), inc=False)
                vop(lambda e, fi=fi, tm=tm: e.tensor_tensor(
                    out=rtmp[64:128, :], in0=qkf[0:64, 512 * fi:512 * fi + 512],
                    in1=rps[tm][0:64, :], op=ALU.mult), inc=False)
                if dst_is_k:
                    dv_ro = vop(lambda e, tb=tb: e.tensor_tensor(
                        out=K_sb[:, 512 * tb:512 * tb + 512], in0=racc[:], in1=rtmp[:],
                        op=ALU.add))
                else:
                    dv_ro = vop(lambda e, qm=qm: e.tensor_tensor(
                        out=qst[qm][:], in0=racc[:], in1=rtmp[:], op=ALU.add))
                    b.wait("sync", DV, dv_ro)
                    qst_sy[qm] = dma("sync", SY, q_scr[128 * fi:128 * fi + 128,
                                                      512 * tb:512 * tb + 512], qst[qm][:])
            dv_vt = vop(lambda e: e.tensor_copy(out=vtmp[:], in_=qkf[:, 2560:3072]))
            b.wait("tensor", DV, dv_vt)
            pst7 = PS[7][:, 0:512].bitcast(BF16)
            dv_tr = 0
            for ts4 in range(4):
                if ts4 > 0:
                    b.wait("tensor", DV, dv_tr)
                pe_tr = b.emit("tensor", lambda e, ts4=ts4, pst7=pst7: e.transpose(
                    pst7[:, 0:128], vtmp[:, 128 * ts4:128 * ts4 + 128], ident[:]), inc=PE)
                b.wait("vector", PE, pe_tr)
                dv_tr = vop(lambda e, tb=tb, ts4=ts4, pst7=pst7: e.tensor_copy(
                    out=V_sb[:, 512 * tb + 128 * ts4:512 * tb + 128 * ts4 + 128],
                    in_=pst7[:, 0:128]))
            rp_dv[tm] = dv_tr
        dv_qkv_done = dv_tr
        qspill_all = b.snap(SY)

        b.wait("tensor", DV, dv_qkv_done)
        STP = PSALL[:, 2048:3072]
        ONESB = PSALL[0:4, 3072:3584]
        BCB = PSALL[:, 3072:3584]
        ac_prev = 0
        ac_expp = {0: 0, 1: 0}
        probs_pe = {0: 0, 1: 0}
        probs_dv = {0: 0, 1: 0}
        dv_at = 0
        at_sy = 0
        pe_tail = 0
        b.wait("sync", SY, qspill_all)
        lq = {}
        lq[0] = dma("sync", SY, q_tile[0][:].rearrange("p (g t) -> p g t", t=512),
                    q_scr[:, 0:512].rearrange("(g p) t -> p g t", p=128))
        for t in range(16):
            if t < 15:
                b.wait("sync", PE, pe_tail)
                lq[(t + 1) % 2] = dma(
                    "sync", SY, q_tile[(t + 1) % 2][:].rearrange("p (g t) -> p g t", t=512),
                    q_scr[:, 512 * (t + 1):512 * (t + 1) + 512]
                    .rearrange("(g p) t -> p g t", p=128))
            qt = q_tile[t % 2]
            b.wait("tensor", SY, lq[t % 2])
            nkb = 4 * (t + 1)
            npair = nkb * 2
            pend = None

            def emit_onespv(pinfo, nkb=nkb):
                pi, infos, par, acv = pinfo
                if pi == 0:
                    b.wait("tensor", DV, dv_at)
                for (si, kb, g, masked) in infos:
                    if masked:
                        b.wait("tensor", DV, probs_dv[par])
                        px = probm[par][:, 512 * si:512 * si + 512]
                    else:
                        b.wait("tensor", AC, acv)
                        px = probp[par][:, 512 * si:512 * si + 512]
                    mm(ONESB, eg[:, 4 * g:4 * g + 4], px,
                       kb == 0 and g == 0, kb == nkb - 1 and g == 3,
                       inc=(kb == nkb - 1 and g == 3))
                    probs_pe[par] = mm(PS[g], V_sb[:, 128 * kb:128 * kb + 128], px,
                                       kb == 0, kb == nkb - 1, inc=True)

            for p in range(npair):
                par = p % 2
                b.wait("tensor", AC, ac_prev)
                for si in range(2):
                    stp = 2 * p + si
                    kb, g = stp // 4, stp % 4
                    mm(STP[:, 512 * si:512 * si + 512], K_sb[:, 128 * kb:128 * kb + 128],
                       qt[:, 512 * g:512 * g + 512], True, True, inc=(si == 1))
                pe_qk = b.snap(PE)
                b.wait("scalar", PE, pe_qk)
                b.wait("scalar", PE, probs_pe[par])
                b.wait("scalar", DV, probs_dv[par])
                ac_prev = aop(lambda e, par=par: e.activation(
                    out=probp[par][:], in_=STP, func=AF.Exp,
                    bias=csts[:, 2:3], scale=SC_ATT))
                ac_expp[par] = ac_prev
                infos = []
                for si in range(2):
                    stp = 2 * p + si
                    kb, g = stp // 4, stp % 4
                    j = kb - 4 * t
                    masked = j >= 0
                    if masked:
                        b.wait("vector", AC, ac_prev)
                        b.wait("vector", PE, probs_pe[par])
                        probs_dv[par] = vop(lambda e, par=par, si=si, j=j: e.tensor_tensor(
                            out=probm[par][:, 512 * si:512 * si + 512],
                            in0=probp[par][:, 512 * si:512 * si + 512],
                            in1=masks[:, 512 * j:512 * j + 512], op=ALU.mult))
                    infos.append((si, kb, g, masked))
                if pend is not None:
                    emit_onespv(pend)
                pend = (p, infos, par, ac_prev)
            emit_onespv(pend)
            pe_tail = b.snap(PE)
            b.wait("vector", PE, pe_tail)
            b.wait("vector", SY, at_sy)
            dv_s = vop(lambda e: e.tensor_copy(out=s_sb[:], in_=ONESB))
            b.wait("vector", DV, dv_s)
            dv_rcp = vop(lambda e: e.reciprocal(rcp[:], s_sb[:]))
            b.wait("tensor", DV, dv_rcp)
            dv_bcg = 0
            for g in range(4):
                if g > 0:
                    b.wait("tensor", DV, dv_bcg)
                pe_bc = mm(BCB, sel4[:, 128 * g:128 * g + 128], rcp[:],
                           True, True, inc=True)
                b.wait("vector", PE, pe_bc)
                dv_bcg = vop(lambda e, g=g: e.tensor_copy(out=bcs[g % 2][:], in_=BCB))
                b.wait("vector", DV, dv_bcg)
                vop(lambda e, g=g: e.tensor_tensor(
                    out=attn_t[:, 512 * g:512 * g + 512], in0=PS[g],
                    in1=bcs[g % 2][:], op=ALU.mult), inc=(g == 3))
            dv_at = b.snap(DV)
            b.wait("sync", DV, dv_at)
            for g in range(4):
                at_sy = dma("sync", SY, at_in[128 * g:128 * g + 128, 512 * t:512 * t + 512],
                            attn_t[:, 512 * g:512 * g + 512])
        b.wait("gpsimd", SY, at_sy)
        cc3 = b.emit("gpsimd", lambda e: e.collective_compute(
            "AllGather", ALU.bypass, ins=[at_in.opt()], outs=[at_all.opt()],
            replica_groups=[list(range(NCORES))]), inc=CC, n=1)
        at_view = at_all.rearrange("e (s t) -> (e s) t", t=512)
        wo_pe = {0: 0, 1: 0, 2: 0, 3: 0}
        st = 0
        dv_wo_prev = dv_at
        pe_gth_gate = 0
        for h in (0, 1):
            sl = slice(512 * h, 512 * h + 512)
            b.wait("tensor", DV, dv_wo_prev)
            b.wait("sync", SY, max(st, state["prev_store"]))
            lxw = dma("sync", SY, xh_w[:].rearrange("p (c t) -> p c t", t=512),
                      src[:, sl].rearrange("(c p) t -> p c t", p=128))
            b.wait("gpsimd", CC, cc3)
            b.wait("gpsimd", PE, pe_gth_gate)
            gth = 0
            for ce in range(NCH):
                gth = b.emit("gpsimd", lambda e, ce=ce, h=h: e.indirect_dma_start(
                    out=at_half[:, 512 * ce:512 * ce + 512], out_offset=None,
                    in_=at_view,
                    in_offset=IndirectOffsetOnAxis(ap=idxwo[:, 2 * ce + h:2 * ce + h + 1],
                                                   axis=0)), inc=GP, n=16)
            b.wait("tensor", GP, gth)
            b.wait("vector", SY, lxw)
            ep_prev = {0: 0, 1: 0, 2: 0, 3: 0}
            for fg in range(8):
                for f4 in range(4):
                    b.wait("tensor", DV, ep_prev[f4])
                for em in range(8):
                    emm = em % 4
                    b.wait("gpsimd", PE, wo_pe[emm])
                    lw = cast(wob[emm][:].rearrange("p (a f) -> p a f", a=4),
                              WO[512 * em:512 * em + 512, 512 * fg:512 * fg + 512]
                              .rearrange("(a p) f -> p a f", p=128))
                    b.wait("tensor", GP, lw)
                    for es in range(4):
                        for f4 in range(4):
                            wo_pe[emm] = mm(
                                PS[f4][:, 0:512],
                                wob[emm][:, 512 * es + 128 * f4:512 * es + 128 * f4 + 128],
                                at_half[:, 512 * (4 * em + es):512 * (4 * em + es) + 512],
                                em == 0 and es == 0, em == 7 and es == 3,
                                inc=(es == 3 and f4 == 3))
                pe_fg = b.snap(PE)
                b.wait("vector", PE, pe_fg)
                for f4 in range(4):
                    f = 4 * fg + f4
                    ep_prev[f4] = vop(lambda e, f=f, f4=f4: e.tensor_tensor(
                        out=xh_w[:, 512 * f:512 * f + 512], in0=xh_w[:, 512 * f:512 * f + 512],
                        in1=PS[f4][:, 0:512], op=ALU.add))
            pe_gth_gate = b.snap(PE)
            dv_wo_prev = b.snap(DV)
            b.wait("sync", DV, dv_wo_prev)
            st = dma("sync", SY, dst[:, sl].rearrange("(c p) t -> p c t", p=128),
                     xh_w[:].rearrange("p (c t) -> p c t", t=512))
        state["dv_tail"] = dv_wo_prev
        state["prev_store"] = st
        return st

    st1 = ffn(X, x_dram if stage >= 2 else Y, s_in[0], s_out[0])
    state["prev_store"] = st1
    if stage >= 2:
        st2 = attention(x_dram, x_dram if stage >= 3 else Y)
        if stage >= 3:
            ffn(x_dram, Y, s_in[1], s_out[1])

    b.wait("sync", SY, b.snap(SY))
    b.wait("gpsimd", GP, b.snap(GP))
    b.wait("vector", DV, b.snap(DV))
    b.wait("scalar", AC, b.snap(AC))
    b.wait("tensor", PE, b.snap(PE))
    b.check()

    with nc.Block() as block:
        b.run_streams(block)
    return nc



def _host_inputs(inputs):
    x = np.asarray(inputs["L0_attn_x"])[0]
    w_in = np.asarray(inputs["L0_attn_w_in"])
    w_out = np.asarray(inputs["L0_attn_w_out"])
    Wq = np.asarray(inputs["L0_ffn_Wq"]); Wr = np.asarray(inputs["L0_ffn_Wr"])
    ak = np.asarray(inputs["L0_ffn_aux_keys"]); av = np.asarray(inputs["L0_ffn_aux_values"])
    Wvd = np.asarray(inputs["L0_ffn_Wvd"]); Wvu = np.asarray(inputs["L0_ffn_Wvu"])
    Wu1 = np.asarray(inputs["L0_ffn_Wu1"]); bu1 = np.asarray(inputs["L0_ffn_bu1"])
    Wu2 = np.asarray(inputs["L0_ffn_Wu2"]); bu2 = np.asarray(inputs["L0_ffn_bu2"])
    gw1 = np.asarray(inputs["L0_ffn_gate_w1"]); gb = np.asarray(inputs["L0_ffn_gate_bias"])
    logrel = np.asarray(inputs["L0_ffn_log_rel"])

    C = np.ascontiguousarray
    inv_freq = 1.0 / (10000.0 ** (np.arange(0, 128, 2, dtype=np.float64) / 128.0))
    ang = np.arange(S, dtype=np.float64)[:, None] * inv_freq[None, :]
    cosv = np.cos(ang).astype(np.float32); sinv = np.sin(ang).astype(np.float32)
    rope_c = np.concatenate([cosv.T, cosv.T], axis=0).astype(BF16NP)
    rope_s = np.concatenate([sinv.T, -sinv.T], axis=0).astype(BF16NP)
    masksm = np.zeros((128, 2048), dtype=np.float32)
    kk = np.arange(128)[:, None]; qq = np.arange(512)[None, :]
    for j in range(4):
        masksm[:, 512 * j:512 * j + 512] = (128 * j + kk <= qq)
    identt = np.eye(128, dtype=np.float32).astype(BF16NP)
    egm = np.zeros((128, 16), dtype=BF16NP)
    for g in range(4):
        egm[:, 4 * g + g] = 1.0
    sel4m = np.zeros((4, 512), dtype=np.float32)
    for g in range(4):
        sel4m[g, 128 * g:128 * g + 128] = 1.0
    cstsm = np.zeros((128, 8), dtype=np.float32)
    cstsm[:, 0] = 1e-5; cstsm[:, 1] = 1.0; cstsm[:, 2] = -BCONST
    wu2c = C(Wu2.reshape(8, 128).T.astype(np.float32))
    bu1cm = C(bu1.reshape(8, 128).T.astype(np.float32))
    sc3m = np.zeros((1, 8), dtype=np.float32)
    sc3m[0, 0] = float(np.reshape(bu2, -1)[0]); sc3m[0, 1] = float(gw1); sc3m[0, 2] = float(gb)

    common = dict(
        w_o=C(w_out.T), wq_t=C(Wq.T), wr_t=C(Wr.T), ak_t=C(ak.T), av=C(av),
        wvd_t=C(Wvd.T), wvu_t=C(Wvu.T), wu1_t=C(Wu1.T),
        wu2_c=wu2c, bu1_c=bu1cm, sc3=sc3m,
        rope_c=rope_c, rope_s=rope_s, masks=masksm.astype(BF16NP), identt=identt,
        eg=egm, sel4=sel4m, onesk=np.ones((1, 128), np.float32),
        oneskb=np.ones((1, 128), BF16NP),
        e8kb=np.full((1, 128), 8.0).astype(BF16NP),
        logrelb=C(logrel.reshape(1, 1024)).astype(BF16NP),
        ones128=np.ones((128, 1), np.float32), csts=cstsm,
    )

    def qrow(j):
        return 1024 * (j // 8) + 128 * (j % 8)

    in_maps = []
    for c in range(NCORES):
        d = dict(common)
        d["x0"] = C(x[TPC * c:TPC * (c + 1)].T)
        wq_rows = np.concatenate([w_in[qrow(4 * c + g):qrow(4 * c + g) + 128]
                                  for g in range(4)], axis=0)
        d["w_q"] = C(wq_rows.T)
        d["w_k"] = C(w_in[4096 + 128 * c:4096 + 128 * c + 128].T)
        d["w_v"] = C(w_in[5120 + 128 * c:5120 + 128 * c + 128].T)
        idx = np.zeros((128, 64), dtype=np.uint32)
        for ce in range(NCH):
            for hh in range(2):
                e = 128 * ce + np.arange(128)
                idx[:, 2 * ce + hh] = (e * 16 + 2 * c + hh).astype(np.uint32)
        d["idx_wo"] = idx
        in_maps.append(d)
    return in_maps


def kernel(**inputs):
    in_maps = _host_inputs(inputs)
    nc = build_program(_STAGE)
    res = run_bass_kernel_spmd(nc, in_maps, core_ids=list(range(NCORES)))
    out = np.empty((1, S, D), dtype=np.float32)
    for c in range(NCORES):
        out[0, TPC * c:TPC * (c + 1), :] = res.results[c]["y"].T
    return out


# version 04
# speedup vs baseline: 1.6428x; 1.0284x over previous
import numpy as np
import ml_dtypes
import concourse.bass as bass
import concourse.mybir as mybir
from concourse.bass import IndirectOffsetOnAxis
from concourse.bass_utils import run_bass_kernel_spmd

F32 = mybir.dt.float32
BF16 = mybir.dt.bfloat16
U32 = mybir.dt.uint32
AF = mybir.ActivationFunctionType
ALU = mybir.AluOpType
AX = mybir.AxisListType
BF16NP = np.dtype(ml_dtypes.bfloat16)

NCORES = 8
S = 8192
D = 4096
TPC = 1024
NCH = 32
BCONST = 30.0
SC_ATT = 1.0 / float(np.sqrt(128.0))
SC_R = 1.0 / float(np.sqrt(48.0))
GC = 0.7978845608028654
GCA = GC * 0.044715

_STAGE = 3

ENGINES = ("tensor", "vector", "scalar", "gpsimd", "sync")


class Builder:
    def __init__(self):
        self.nc = bass.Bass("TRN2", target_bir_lowering=False)
        self.stream = {e: [] for e in ENGINES}
        self.sems = {}
        self.cnt = {}
        self.items = []
        for s in ("SY", "SC", "GP", "PE", "DV", "AC", "CC"):
            self.sems[s] = self.nc.alloc_semaphore(s)
            self.cnt[s] = 0
        self.SBUF_BASE = 16512
        self.SBUF_MAX = 229344

    def emit(self, eng, fn, inc=None, n=1):
        rec = {"eng": eng, "pos": len(self.stream[eng]), "waits": [], "incs": []}
        if inc is not None:
            self.cnt[inc] += n
            rec["incs"].append((inc, self.cnt[inc]))
            h = self.sems[inc]

            def wrapped(e, fn=fn, h=h, n=n):
                fn(e).then_inc(h, n)

            self.stream[eng].append(wrapped)
        else:
            self.stream[eng].append(fn)
        self.items.append(rec)
        return self.cnt[inc] if inc else None

    def wait(self, eng, sem, val):
        if val <= 0:
            return
        h = self.sems[sem]
        rec = {"eng": eng, "pos": len(self.stream[eng]), "waits": [(sem, val)], "incs": []}
        self.stream[eng].append(lambda e, h=h, v=val: e.wait_ge(h, v))
        self.items.append(rec)

    def snap(self, sem):
        return self.cnt[sem]

    def at(self, name, shape, dtype, off):
        esz = 4 if dtype in (F32, U32) else 2
        nbytes = int(np.prod(shape[1:])) * esz
        assert off >= self.SBUF_BASE and off + nbytes <= self.SBUF_MAX, \
            f"{name}: [{off}, {off}+{nbytes}) outside [{self.SBUF_BASE}, {self.SBUF_MAX})"
        return self.nc.alloc_sbuf_tensor_at(name, list(shape), dtype, offset=off).ap()

    def check(self):
        final = dict(self.cnt)
        producers = {}
        for i, rec in enumerate(self.items):
            for sem, cum in rec["incs"]:
                producers.setdefault(sem, []).append((cum, i))
        n = len(self.items)
        adj = [[] for _ in range(n)]
        laste = {}
        for i, rec in enumerate(self.items):
            e = rec["eng"]
            if e in laste:
                adj[laste[e]].append(i)
            laste[e] = i
            for sem, v in rec["waits"]:
                assert v <= final[sem], f"wait {sem}>={v} but final={final[sem]}"
                src = None
                for cum, j in producers.get(sem, []):
                    if cum >= v:
                        src = j
                        break
                assert src is not None, (sem, v)
                adj[src].append(i)
        state = [0] * n
        order_pos = {}
        stack = []
        for s0 in range(n):
            if state[s0]:
                continue
            stack.append((s0, iter(adj[s0])))
            state[s0] = 1
            while stack:
                node, it = stack[-1]
                advanced = False
                for nxt in it:
                    if state[nxt] == 1:
                        raise RuntimeError(
                            f"sync cycle: item {nxt} eng={self.items[nxt]['eng']} "
                            f"pos={self.items[nxt]['pos']} waits={self.items[nxt]['waits']}")
                    if state[nxt] == 0:
                        state[nxt] = 1
                        stack.append((nxt, iter(adj[nxt])))
                        advanced = True
                        break
                if not advanced:
                    state[node] = 2
                    stack.pop()

    def run_streams(self, block):
        for eng in ENGINES:
            getattr(block, eng)(lambda e, eng=eng: [f(e) for f in self.stream[eng]])


def build_program(stage):
    b = Builder()
    nc = b.nc
    SY, SC, GP, PE, DV, AC, CC = "SY", "SC", "GP", "PE", "DV", "AC", "CC"

    dram = lambda n, s, d, k="ExternalInput": nc.dram_tensor(n, list(s), d, kind=k).ap()
    X = dram("x0", [D, TPC], F32)
    WQ = dram("w_q", [D, 512], F32)
    WK = dram("w_k", [D, 128], F32)
    WV = dram("w_v", [D, 128], F32)
    WO = dram("w_o", [D, D], F32)
    WQT = dram("wq_t", [D, 64], F32)
    WRT = dram("wr_t", [64, 48], F32)
    AKT = dram("ak_t", [64, 1024], F32)
    AV = dram("av", [1024, D], F32)
    WVDT = dram("wvd_t", [D, 256], F32)
    WVUT = dram("wvu_t", [256, D], F32)
    WU1T = dram("wu1_t", [D, 1024], F32)
    WU2C = dram("wu2_c", [128, 8], F32)
    BU1C = dram("bu1_c", [128, 8], F32)
    SC3 = dram("sc3", [1, 8], F32)
    ROPEC = dram("rope_c", [128, S], BF16)
    ROPES = dram("rope_s", [128, S], BF16)
    MASKS = dram("masks", [128, 2048], BF16)
    IDENT = dram("identt", [128, 128], BF16)
    EG = dram("eg", [128, 16], BF16)
    SEL4 = dram("sel4", [4, 512], F32)
    ONESK = dram("onesk", [1, 128], F32)
    ONESKB = dram("oneskb", [1, 128], BF16)
    E8KB = dram("e8kb", [1, 128], BF16)
    LOGRELB = dram("logrelb", [1, 1024], BF16)
    ONES128 = dram("ones128", [128, 1], F32)
    CSTS = dram("csts", [128, 8], F32)
    IDXWO = dram("idx_wo", [128, 64], U32)
    IDXWO1 = dram("idx_wo1", [128, 64], U32)

    Y = dram("y", [D, TPC], F32, "ExternalOutput")

    x_dram = dram("x_scr", [D, TPC], F32, "Internal")
    xh_in = [dram(f"xh_in{i}", [D, 512], BF16, "Internal") for i in range(2)]
    xh_all = [dram(f"xh_all{i}", [NCORES * D, 512], BF16, "Internal") for i in range(2)]
    q_scr = dram("q_scr", [512, S], BF16, "Internal")
    at_in = [dram(f"at_in{i}", [512, S // 2], BF16, "Internal") for i in range(2)]
    at_all = [dram(f"at_all{i}", [D, S // 2], BF16, "Internal") for i in range(2)]
    s_in = [dram(f"s{i}_in", [1, 1], F32, "Internal") for i in range(2)]
    s_out = [dram(f"s{i}_out", [8, 1], F32, "Internal") for i in range(2)]

    o = [b.SBUF_BASE]
    def P(name, shape, dtype):
        ap = b.at(name, shape, dtype, o[0])
        esz = 4 if dtype in (F32, U32) else 2
        o[0] += (int(np.prod(shape[1:])) * esz + 31) // 32 * 32
        return ap

    m_row = P("m_row", [1, 1024], F32)
    rstd_row = P("rstd_row", [1, 1024], F32)
    logv_row = P("logv_row", [1, 1024], F32)
    rt1 = P("rt1", [1, 512], F32)
    rt2 = P("rt2", [1, 512], F32)
    rt3 = P("rt3", [1, 512], F32)
    gate_row = P("gate_row", [1, 1024], F32)
    c2_row = P("c2_row", [1, 1024], F32)
    gsc = P("gsc", [1, 16], F32)
    sc3 = P("sc3_sb", [1, 8], F32)
    csts = P("csts_sb", [128, 8], F32)
    onesk = P("onesk_sb", [1, 128], F32)
    oneskb = P("oneskb_sb", [1, 128], BF16)
    e8kb = P("e8kb_sb", [1, 128], BF16)
    logrelb = P("logrelb_sb", [1, 1024], BF16)
    c2bf = P("c2bf", [1, 512], BF16)
    ones128 = P("ones128_sb", [128, 1], F32)
    idxwo = P("idxwo_sb", [128, 64], U32)
    idxwo2 = [idxwo, P("idxwo1_sb", [128, 64], U32)]
    top8 = P("top8", [128, 8], F32)
    colA = P("colA", [128, 1], F32)
    colB = P("colB", [128, 1], F32)
    colC = P("colC", [128, 1], F32)
    colD = P("colD", [128, 1], F32)
    wu2c = P("wu2c_sb", [128, 8], BF16)
    bu1c = P("bu1c_sb", [128, 8], F32)
    eg = P("eg_sb", [128, 16], BF16)
    ident = P("ident_sb", [128, 128], BF16)
    sel4 = P("sel4_sb", [4, 512], F32)
    masks = P("masks_sb", [128, 2048], BF16)
    ak_bf = P("ak_bf", [64, 1024], BF16)
    rk_bf = P("rk_bf", [48, 1024], BF16)
    wr_bf = P("wr_bf", [64, 48], BF16)
    wq_bf = P("wq_bf", [128, 2048], BF16)
    WBASE = (o[0] + 63) // 64 * 64

    off = WBASE
    def nxt(nbytes):
        nonlocal off
        r = off
        off += (nbytes + 31) // 32 * 32
        return r
    O_XH = nxt(32 * 512 * 4)
    O_H = nxt(32 * 512 * 2)
    O_SQ0 = nxt(512 * 4)
    O_SQ1 = nxt(512 * 4)
    O_WU1 = [nxt(1024 * 2) for _ in range(4)]
    O_U1 = nxt(8 * 512 * 2)
    O_RT = nxt(1024 * 4)
    O_TT = nxt(1024 * 4)
    O_WTK = nxt(1024 * 4)
    O_WBF = nxt(1024 * 2)
    O_GX = nxt(512 * 4)
    O_G1 = nxt(512 * 4)
    O_G2 = nxt(512 * 4)
    O_BCA = nxt(512 * 4)
    O_BCB = nxt(512 * 4)
    O_AVB = nxt(4 * 512 * 2)
    O_WVDB = nxt(2 * 4 * 256 * 2)
    O_A1 = nxt(4 * 512 * 2)
    O_A2 = nxt(2 * 512 * 2)
    assert off <= b.SBUF_MAX, off

    xh = b.at("xh", [128, 32 * 512], F32, O_XH)
    h_sb = b.at("h_sb", [128, 32 * 512], BF16, O_H)
    sqb = [b.at("sq0", [128, 512], F32, O_SQ0), b.at("sq1", [128, 512], F32, O_SQ1)]
    wu1b = [b.at(f"wu1b{i}", [128, 1024], BF16, O_WU1[i]) for i in range(4)]
    u1sb = b.at("u1sb", [128, 8 * 512], BF16, O_U1)
    Rt = b.at("Rt", [128, 1024], F32, O_RT)
    Tt = b.at("Tt", [128, 1024], F32, O_TT)
    wtok = b.at("wtok", [128, 1024], F32, O_WTK)
    wbf = b.at("wbf", [128, 1024], BF16, O_WBF)
    gx = b.at("gx", [128, 512], F32, O_GX)
    g1 = b.at("g1", [128, 512], F32, O_G1)
    g2 = b.at("g2", [128, 512], F32, O_G2)
    bcA = b.at("bcA", [128, 512], F32, O_BCA)
    bcB = b.at("bcB", [128, 512], F32, O_BCB)
    avb = [b.at(f"avb{i}", [128, 512], BF16, O_AVB + 1024 * i) for i in range(4)]
    wvdb = [b.at("wvdb0", [128, 4 * 256], BF16, O_WVDB),
            b.at("wvdb1", [128, 4 * 256], BF16, O_WVDB + 2048)]
    a1sb = b.at("a1sb", [128, 4 * 512], BF16, O_A1)
    a2sb = b.at("a2sb", [128, 2 * 512], BF16, O_A2)
    qryT = b.at("qryT", [64, 512], BF16, O_G1)
    rqT = b.at("rqT", [48, 512], BF16, O_G2)
    wT = b.at("wT", [128, 8 * 512], BF16, O_SQ0)
    gxp = [gx, b.at("gx1", [128, 512], F32, O_RT)]
    g1p = [g1, b.at("g11", [128, 512], F32, O_RT + 2048)]
    g2p = [g2, b.at("g21", [128, 512], F32, O_WBF)]
    wvub = b.at("wvub", [128, 2 * 4096], BF16, O_U1)

    import contextlib
    pctx = contextlib.ExitStack()
    PSALL = pctx.enter_context(nc.psum_tensor("psall", [128, 4096], F32))
    PS = [PSALL[:, 512 * i:512 * i + 512] for i in range(8)]

    def dma(qeng, qsem, out, in_):
        return b.emit(qeng, lambda e, out=out, in_=in_: e.dma_start(out=out, in_=in_),
                      inc=qsem, n=16)

    def mm(out, lhsT, rhs, start, stop, inc=False):
        return b.emit("tensor",
                      lambda e, out=out, l=lhsT, r=rhs, s1=start, s2=stop:
                      e.matmul(out, l, r, start=s1, stop=s2),
                      inc=(PE if inc else None)) or b.snap(PE)

    def vop(fn, inc=True):
        return b.emit("vector", fn, inc=(DV if inc else None)) or b.snap(DV)

    def aop(fn, inc=True):
        return b.emit("scalar", fn, inc=(AC if inc else None)) or b.snap(AC)


    dma("sync", SY, csts, CSTS)
    dma("sync", SY, sc3, SC3)
    dma("sync", SY, onesk, ONESK)
    dma("sync", SY, oneskb, ONESKB)
    dma("sync", SY, e8kb, E8KB)
    dma("sync", SY, logrelb, LOGRELB)
    dma("sync", SY, ones128, ONES128)
    dma("sync", SY, idxwo, IDXWO)
    dma("sync", SY, idxwo2[1], IDXWO1)
    cast = lambda out, in_: dma("gpsimd", "GP", out, in_)
    cast(wu2c, WU2C)
    dma("sync", SY, bu1c, BU1C)
    dma("sync", SY, eg, EG)
    dma("sync", SY, ident, IDENT)
    dma("sync", SY, sel4, SEL4)
    consts_sy = dma("sync", SY, masks, MASKS)
    cast(ak_bf, AKT)
    cast(wr_bf, WRT)
    consts_gp = cast(wq_bf[:].rearrange("p (c n) -> p c n", n=64),
                     WQT.rearrange("(c p) n -> p c n", p=128))
    b.wait("tensor", "GP", consts_gp)

    b.wait("vector", GP, consts_gp)

    mm(PS[0][0:48, 0:512], wr_bf[:], ak_bf[:, 0:512], True, True)
    pe_rk = mm(PS[1][0:48, 0:512], wr_bf[:], ak_bf[:, 512:1024], True, True, inc=True)
    b.wait("vector", PE, pe_rk)
    vop(lambda e: e.tensor_scalar_mul(rk_bf[:, 0:512], PS[0][0:48, 0:512], SC_R), inc=False)
    dv_rk = vop(lambda e: e.tensor_scalar_mul(rk_bf[:, 512:1024], PS[1][0:48, 0:512], SC_R))

    state = {"prev_store": 0, "dv_rk": dv_rk, "dv_tail": 0}


    def load_xhalf(src_t, sl, buf):
        outs = []
        bv = buf[:].rearrange("p (c t) -> p c t", t=512)
        for pz in range(4):
            outs.append(dma("sync", SY, bv[:, 8 * pz:8 * pz + 8, :],
                            src_t[1024 * pz:1024 * pz + 1024, sl]
                            .rearrange("(c p) t -> p c t", p=128)))
        return outs

    def ffn(src, dst, sin_d, sout_d):
        b.wait("tensor", DV, state["dv_tail"])
        b.wait("sync", SY, state["prev_store"])
        stat_pe = {}
        sq_pe = {0: 0, 1: 0}
        ac_sq = 0
        for h in (0, 1):
            if h == 1:
                b.wait("sync", PE, stat_pe[0])
                b.wait("sync", AC, ac_sq)
            lds = load_xhalf(src, slice(512 * h, 512 * h + 512), xh)
            if h == 0:
                b.wait("tensor", DV, state["dv_rk"])
            for c in range(NCH):
                if c % 8 == 0:
                    b.wait("scalar", SY, lds[c // 8])
                    b.wait("tensor", SY, lds[c // 8])
                if c >= 2 or h == 1:
                    b.wait("scalar", PE, sq_pe[c % 2])
                ac_sq = aop(lambda e, c=c: e.activation(
                    out=sqb[c % 2][:], in_=xh[:, 512 * c:512 * c + 512], func=AF.Square))
                b.wait("tensor", AC, ac_sq)
                mm(PS[0 + h][0:1, 0:512], ones128[:], xh[:, 512 * c:512 * c + 512],
                   c == 0, c == NCH - 1)
                sq_pe[c % 2] = mm(PS[2 + h][0:1, 0:512], ones128[:], sqb[c % 2][:],
                                  c == 0, c == NCH - 1, inc=True)
            stat_pe[h] = b.snap(PE)
        b.wait("vector", PE, stat_pe[1])
        ac_logv = 0
        for h in (0, 1):
            sl = slice(512 * h, 512 * h + 512)
            if h == 1:
                b.wait("vector", AC, ac_logv)
            vop(lambda e, sl=sl: e.tensor_scalar_mul(m_row[0:1, sl], PS[0 + h][0:1, 0:512],
                                                     1.0 / D), inc=False)
            dv_m = vop(lambda e: e.engine_nop())
            b.wait("vector", DV, dv_m)
            vop(lambda e, sl=sl: e.tensor_tensor(out=rt1[0:1, 0:512], in0=m_row[0:1, sl],
                                                 in1=m_row[0:1, sl], op=ALU.mult), inc=False)
            dv_v = vop(lambda e, h=h: e.scalar_tensor_tensor(
                out=rt2[0:1, 0:512], in0=PS[2 + h][0:1, 0:512], scalar=1.0 / D,
                in1=rt1[0:1, 0:512], op0=ALU.mult, op1=ALU.subtract))
            b.wait("scalar", DV, dv_v)
            ac_ln = aop(lambda e: e.activation(out=rt3[0:1, 0:512], in_=rt2[0:1, 0:512],
                                               func=AF.Ln, bias=csts[0:1, 0:1]))
            b.wait("scalar", AC, ac_ln)
            ac_rstd = aop(lambda e, sl=sl: e.activation(out=rstd_row[0:1, sl],
                                                        in_=rt3[0:1, 0:512],
                                                        func=AF.Exp, scale=-0.5))
            b.wait("vector", AC, ac_ln)
            vop(lambda e: e.tensor_scalar_add(rt1[0:1, 0:512], rt2[0:1, 0:512], 1e-5),
                inc=False)
            dv_a = vop(lambda e: e.engine_nop())
            b.wait("vector", DV, dv_a)
            dv_b = vop(lambda e: e.reciprocal(rt1[0:1, 0:512], rt1[0:1, 0:512]))
            b.wait("vector", DV, dv_b)
            dv_vh = vop(lambda e: e.tensor_tensor(out=rt1[0:1, 0:512], in0=rt2[0:1, 0:512],
                                                  in1=rt1[0:1, 0:512], op=ALU.mult))
            b.wait("scalar", DV, dv_vh)
            ac_logv = aop(lambda e, sl=sl: e.activation(out=logv_row[0:1, sl],
                                                        in_=rt1[0:1, 0:512], func=AF.Ln,
                                                        bias=csts[0:1, 1:2]))
        b.wait("vector", AC, ac_logv)
        dv_part = vop(lambda e: e.reduce_sum(out=gsc[0:1, 0:1], in_=logv_row[:], axis=AX.X))
        b.wait("gpsimd", DV, dv_part)
        gp_s = dma("gpsimd", GP, sin_d, gsc[0:1, 0:1])
        b.wait("gpsimd", GP, gp_s)
        cc_done = b.emit("gpsimd", lambda e: e.collective_compute(
            "AllGather", ALU.bypass, ins=[sin_d.opt()], outs=[sout_d.opt()],
            replica_groups=[list(range(NCORES))]), inc=CC, n=1)
        b.wait("gpsimd", CC, cc_done)
        gp_ag = dma("gpsimd", GP, gsc[0:1, 4:12], sout_d.rearrange("a b -> b a"))
        b.wait("vector", GP, gp_ag)
        vop(lambda e: e.reduce_sum(out=gsc[0:1, 1:2], in_=gsc[0:1, 4:12], axis=AX.X), inc=False)
        dv_tot = vop(lambda e: e.engine_nop())
        b.wait("vector", DV, dv_tot)
        vop(lambda e: e.tensor_scalar(out=gsc[0:1, 2:3], in0=gsc[0:1, 1:2], scalar1=1.0 / S,
                                      scalar2=1e-6, op0=ALU.mult, op1=ALU.add), inc=False)
        dv_t2 = vop(lambda e: e.engine_nop())
        b.wait("vector", DV, dv_t2)
        dv_invm = vop(lambda e: e.reciprocal(gsc[0:1, 2:3], gsc[0:1, 2:3]))

        wu1_pe = {0: 0, 1: 0, 2: 0, 3: 0}
        av_pe = {0: 0, 1: 0, 2: 0, 3: 0}
        wvd_pe = {0: 0, 1: 0}
        st = 0
        for hi, h in enumerate((1, 0)):
            sl = slice(512 * h, 512 * h + 512)
            if hi == 1:
                b.wait("tensor", DV, dv_ep_prev)
                b.wait("sync", SY, st)
                lds2 = load_xhalf(src, sl, xh)
                b.wait("vector", SY, lds2[3])
                b.wait("tensor", SY, lds2[3])
            b.wait("tensor", AC, ac_rstd)
            mm(PS[5][:, 0:512], onesk[:], m_row[0:1, sl], True, True)
            pe_bc = mm(PS[6][:, 0:512], onesk[:], rstd_row[0:1, sl], True, True, inc=True)
            b.wait("vector", PE, pe_bc)
            vop(lambda e: e.tensor_copy(out=bcA[:], in_=PS[5][:, 0:512]), inc=False)
            dv_bc = vop(lambda e: e.tensor_copy(out=bcB[:], in_=PS[6][:, 0:512]))
            for c in range(NCH):
                vop(lambda e, c=c: e.tensor_tensor(out=gx[:], in0=xh[:, 512 * c:512 * c + 512],
                                                   in1=bcA[:], op=ALU.subtract), inc=False)
                vop(lambda e, c=c: e.tensor_tensor(out=h_sb[:, 512 * c:512 * c + 512],
                                                   in0=gx[:], in1=bcB[:], op=ALU.mult),
                    inc=(c == NCH - 1))
            dv_h = b.snap(DV)
            b.wait("tensor", DV, dv_h)
            for c in range(NCH):
                cm = c % 4
                b.wait("gpsimd", PE, wu1_pe[cm])
                ldw = cast(wu1b[cm][:], WU1T[128 * c:128 * c + 128, :])
                b.wait("tensor", GP, ldw)
                for u in range(8):
                    wu1_pe[cm] = mm(PS[u][:, 0:512], wu1b[cm][:, 128 * u:128 * u + 128],
                                    h_sb[:, 512 * c:512 * c + 512], c == 0, c == NCH - 1,
                                    inc=(u == 7))
            pe_u1 = b.snap(PE)
            b.wait("vector", PE, pe_u1)
            ac2u = {0: 0, 1: 0}
            acc1u = {0: 0, 1: 0}
            for u in range(8):
                um = u % 2
                gxu, g1u, g2u = gxp[um], g1p[um], g2p[um]
                b.wait("vector", AC, ac2u[um])
                dv0 = vop(lambda e, u=u, gxu=gxu: e.tensor_scalar(
                    out=gxu[:], in0=PS[u][:, 0:512], scalar1=bu1c[:, u:u + 1],
                    scalar2=None, op0=ALU.add))
                b.wait("scalar", DV, dv0)
                acc1u[um] = aop(lambda e, gxu=gxu, g1u=g1u: e.activation(
                    out=g1u[:], in_=gxu[:], func=AF.Square))
                b.wait("vector", AC, acc1u[um])
                vop(lambda e, g1u=g1u: e.tensor_scalar(out=g1u[:], in0=g1u[:], scalar1=GCA,
                                                       scalar2=GC, op0=ALU.mult, op1=ALU.add),
                    inc=False)
                dv1 = vop(lambda e, g1u=g1u, gxu=gxu: e.tensor_tensor(
                    out=g1u[:], in0=g1u[:], in1=gxu[:], op=ALU.mult))
                b.wait("scalar", DV, dv1)
                ac2u[um] = aop(lambda e, g1u=g1u, g2u=g2u: e.activation(
                    out=g2u[:], in_=g1u[:], func=AF.Tanh))
                b.wait("vector", AC, ac2u[um])
                vop(lambda e, g2u=g2u: e.tensor_scalar(out=g2u[:], in0=g2u[:], scalar1=0.5,
                                                       scalar2=0.5, op0=ALU.mult, op1=ALU.add),
                    inc=False)
                dv_prev = vop(lambda e, u=u, g2u=g2u, gxu=gxu: e.tensor_tensor(
                    out=u1sb[:, 512 * u:512 * u + 512], in0=g2u[:], in1=gxu[:], op=ALU.mult))
            dv_gelu = dv_prev
            b.wait("tensor", DV, dv_gelu)
            for u in range(8):
                mm(PS[0][0:1, 0:512], wu2c[:, u:u + 1], u1sb[:, 512 * u:512 * u + 512],
                   u == 0, u == 7, inc=(u == 7))
            pe_lr = b.snap(PE)
            b.wait("vector", PE, pe_lr)
            vop(lambda e: e.tensor_scalar(out=rt1[0:1, 0:512], in0=PS[0][0:1, 0:512],
                                          scalar1=sc3[0:1, 0:1], scalar2=None, op0=ALU.add),
                inc=False)
            dv_l = vop(lambda e: e.engine_nop())
            b.wait("scalar", DV, dv_l)
            ac_sg = aop(lambda e: e.activation(out=rt2[0:1, 0:512], in_=rt1[0:1, 0:512],
                                               func=AF.Tanh, scale=0.5))
            b.wait("vector", AC, ac_sg)
            b.wait("vector", DV, dv_invm)
            vop(lambda e: e.tensor_scalar(out=rt2[0:1, 0:512], in0=rt2[0:1, 0:512], scalar1=1.25,
                                          scalar2=1.25, op0=ALU.mult, op1=ALU.add), inc=False)
            vop(lambda e, sl=sl: e.tensor_scalar(out=rt3[0:1, 0:512], in0=logv_row[0:1, sl],
                                                 scalar1=gsc[0:1, 2:3], scalar2=0.5,
                                                 op0=ALU.mult, op1=ALU.mult), inc=False)
            vop(lambda e: e.tensor_tensor(out=rt2[0:1, 0:512], in0=rt2[0:1, 0:512],
                                          in1=rt3[0:1, 0:512], op=ALU.add), inc=False)
            vop(lambda e: e.tensor_scalar(out=rt2[0:1, 0:512], in0=rt2[0:1, 0:512], scalar1=0.0,
                                          scalar2=5.0, op0=ALU.max, op1=ALU.min), inc=False)
            vop(lambda e: e.tensor_scalar(out=rt2[0:1, 0:512], in0=rt2[0:1, 0:512], scalar1=0.5,
                                          scalar2=1.0 / 1.5, op0=ALU.subtract, op1=ALU.mult),
                inc=False)
            vop(lambda e: e.tensor_scalar(out=rt2[0:1, 0:512], in0=rt2[0:1, 0:512], scalar1=0.0,
                                          scalar2=1.0, op0=ALU.max, op1=ALU.min), inc=False)
            vop(lambda e: e.tensor_scalar(out=rt2[0:1, 0:512], in0=rt2[0:1, 0:512],
                                          scalar1=sc3[0:1, 1:2], scalar2=sc3[0:1, 2:3],
                                          op0=ALU.mult, op1=ALU.add), inc=False)
            dv_z = vop(lambda e: e.engine_nop())
            b.wait("scalar", DV, dv_z)
            ac_g = aop(lambda e: e.activation(out=rt3[0:1, 0:512], in_=rt2[0:1, 0:512],
                                              func=AF.Tanh, scale=0.5))
            b.wait("vector", AC, ac_g)
            vop(lambda e: e.tensor_scalar(out=rt3[0:1, 0:512], in0=rt3[0:1, 0:512], scalar1=0.5,
                                          scalar2=0.5, op0=ALU.mult, op1=ALU.add), inc=False)
            dv_g1 = vop(lambda e: e.engine_nop())
            b.wait("vector", DV, dv_g1)
            vop(lambda e: e.tensor_scalar(out=rt2[0:1, 0:512], in0=rt3[0:1, 0:512], scalar1=0.05,
                                          scalar2=None, op0=ALU.is_ge), inc=False)
            vop(lambda e: e.tensor_tensor(out=rt3[0:1, 0:512], in0=rt3[0:1, 0:512],
                                          in1=rt2[0:1, 0:512], op=ALU.mult), inc=False)
            dv_gate = vop(lambda e, sl=sl: e.tensor_scalar_mul(gate_row[0:1, sl],
                                                               rt3[0:1, 0:512], 0.5))

            for c in range(NCH):
                mm(PS[7][0:64, 0:512], wq_bf[:, 64 * c:64 * c + 64],
                   h_sb[:, 512 * c:512 * c + 512], c == 0, c == NCH - 1, inc=(c == NCH - 1))
            pe_q = b.snap(PE)
            b.wait("vector", PE, pe_q)
            dv_q = vop(lambda e: e.tensor_copy(out=qryT[:], in_=PS[7][0:64, 0:512]))
            b.wait("tensor", DV, dv_q)
            pe_rq = mm(PS[6][0:48, 0:512], wr_bf[:], qryT[:], True, True, inc=True)
            b.wait("vector", PE, pe_rq)
            dv_rq = vop(lambda e: e.tensor_scalar_mul(rqT[:], PS[6][0:48, 0:512], SC_R))
            b.wait("tensor", DV, dv_rq)
            b.wait("tensor", DV, dv_gate)

            dv_cd = 0
            pe_tr_last = 0
            for t4 in range(4):
                ts0 = 128 * t4
                if t4 > 0:
                    b.wait("tensor", DV, dv_cd)
                mm(PS[0][:, 0:512], rqT[:, ts0:ts0 + 128], rk_bf[:, 0:512], True, False)
                mm(PS[0][:, 0:512], oneskb[:], logrelb[0:1, 0:512], False, True)
                mm(PS[1][:, 0:512], rqT[:, ts0:ts0 + 128], rk_bf[:, 512:1024], True, False)
                mm(PS[1][:, 0:512], oneskb[:], logrelb[0:1, 512:1024], False, True)
                mm(PS[2][:, 0:512], qryT[:, ts0:ts0 + 128], ak_bf[:, 0:512], True, False)
                mm(PS[2][:, 0:512], e8kb[:], logrelb[0:1, 0:512], False, True)
                mm(PS[3][:, 0:512], qryT[:, ts0:ts0 + 128], ak_bf[:, 512:1024], True, False)
                mm(PS[3][:, 0:512], e8kb[:], logrelb[0:1, 512:1024], False, True)
                pe_r = mm(PS[4][:, 0:1], gate_row[0:1, 512 * h + ts0:512 * h + ts0 + 128],
                          onesk[0:1, 0:1], True, True, inc=True)
                b.wait("vector", PE, pe_r)
                dv_m8 = vop(lambda e: e.max(out=top8[:], in_=PSALL[:, 0:1024]))
                b.wait("vector", DV, dv_m8)
                dv_R = vop(lambda e: e.tensor_scalar(out=wtok[:], in0=PSALL[:, 0:1024],
                                                     scalar1=top8[:, 7:8],
                                                     scalar2=None, op0=ALU.is_lt))
                vop(lambda e: e.scalar_tensor_tensor(out=Tt[:, 0:512], in0=wtok[:, 0:512],
                                                     scalar=-1e30, in1=PS[2][:, 0:512],
                                                     op0=ALU.mult, op1=ALU.add), inc=False)
                dv_T = vop(lambda e: e.scalar_tensor_tensor(
                    out=Tt[:, 512:1024], in0=wtok[:, 512:1024], scalar=-1e30,
                    in1=PS[3][:, 0:512], op0=ALU.mult, op1=ALU.add))
                b.wait("vector", DV, dv_T)
                dv_mx = vop(lambda e: e.reduce_max(out=colA[:], in_=Tt[:], axis=AX.X))
                b.wait("vector", DV, dv_mx)
                dv_nb = vop(lambda e: e.tensor_scalar_mul(colB[:], colA[:], -0.125))
                b.wait("scalar", DV, dv_nb)
                ac_e = aop(lambda e: e.activation(out=wtok[:], in_=Tt[:], func=AF.Exp,
                                                  bias=colB[:], scale=0.125))
                b.wait("vector", AC, ac_e)
                dv_s8 = vop(lambda e: e.reduce_sum(out=colC[:], in_=wtok[:], axis=AX.X))
                b.wait("vector", DV, dv_s8)
                dv_r8 = vop(lambda e: e.reciprocal(colC[:], colC[:]))
                b.wait("vector", DV, dv_r8)
                dv_cd = vop(lambda e: e.tensor_tensor(out=colD[:], in0=PS[4][:, 0:1],
                                                      in1=colC[:], op=ALU.mult))
                b.wait("vector", DV, dv_cd)
                b.wait("vector", PE, pe_tr_last)
                dv_wb = vop(lambda e: e.tensor_scalar(out=wbf[:], in0=wtok[:], scalar1=colD[:],
                                                      scalar2=None, op0=ALU.mult))
                b.wait("tensor", DV, dv_wb)
                pst = PS[5][:, 0:512].bitcast(BF16)
                dv_tr = 0
                for n in range(8):
                    if n > 0:
                        b.wait("tensor", DV, dv_tr)
                    pe_tr_last = b.emit("tensor",
                                        lambda e, n=n, pst=pst: e.transpose(
                                            pst[:, 0:128], wbf[:, 128 * n:128 * n + 128], ident[:]),
                                        inc=PE)
                    b.wait("vector", PE, pe_tr_last)
                    dv_tr = vop(lambda e, n=n, ts0=ts0, pst=pst: e.tensor_copy(
                        out=wT[:, 512 * n + ts0:512 * n + ts0 + 128], in_=pst[:, 0:128]))
            dv_wT = dv_tr

            b.wait("tensor", DV, dv_wT)
            for g in range(8):
                gm = g % 2
                b.wait("gpsimd", PE, wvd_pe[gm])
                ldw = cast(wvdb[gm][:].rearrange("p (a n) -> p a n", a=4),
                           WVDT[512 * g:512 * g + 512, :].rearrange("(a p) n -> p a n", p=128))
                for n in range(8):
                    nm = n % 4
                    b.wait("gpsimd", PE, av_pe[nm])
                    lda = cast(avb[nm][:], AV[128 * n:128 * n + 128, 512 * g:512 * g + 512])
                    b.wait("tensor", GP, lda)
                    for u4 in range(4):
                        av_pe[nm] = mm(PS[u4][:, 0:512], avb[nm][:, 128 * u4:128 * u4 + 128],
                                       wT[:, 512 * n:512 * n + 512], n == 0, n == 7,
                                       inc=(u4 == 3))
                pe_a1 = b.snap(PE)
                b.wait("vector", PE, pe_a1)
                for u4 in range(4):
                    vop(lambda e, u4=u4: e.tensor_copy(out=a1sb[:, 512 * u4:512 * u4 + 512],
                                                       in_=PS[u4][:, 0:512]),
                        inc=(u4 == 3))
                dv_a1 = b.snap(DV)
                b.wait("tensor", DV, dv_a1)
                b.wait("tensor", GP, ldw)
                for u4 in range(4):
                    for v2 in range(2):
                        wvd_pe[gm] = mm(
                            PS[4 + v2][:, 0:512],
                            wvdb[gm][:, 256 * u4 + 128 * v2:256 * u4 + 128 * v2 + 128],
                            a1sb[:, 512 * u4:512 * u4 + 512],
                            (g == 0 and u4 == 0), (g == 7 and u4 == 3),
                            inc=(u4 == 3 and v2 == 1))
            pe_a2 = b.snap(PE)
            b.wait("vector", PE, pe_a2)
            vop(lambda e: e.tensor_copy(out=a2sb[:, 0:512], in_=PS[4][:, 0:512]), inc=False)
            dv_a2 = vop(lambda e: e.tensor_copy(out=a2sb[:, 512:1024], in_=PS[5][:, 0:512]))
            vop(lambda e, sl=sl: e.tensor_tensor(out=c2_row[0:1, sl], in0=m_row[0:1, sl],
                                                 in1=rstd_row[0:1, sl], op=ALU.mult), inc=False)
            dv_c2a = vop(lambda e: e.engine_nop())
            b.wait("vector", DV, dv_c2a)
            vop(lambda e, sl=sl: e.tensor_scalar_mul(c2_row[0:1, sl], c2_row[0:1, sl], -0.5),
                inc=False)
            vop(lambda e, sl=sl: e.tensor_copy(out=c2bf[:], in_=c2_row[0:1, sl]), inc=False)
            dv_c1 = vop(lambda e, sl=sl: e.tensor_scalar(
                out=rt1[0:1, 0:512], in0=rstd_row[0:1, sl], scalar1=0.5, scalar2=1.0,
                op0=ALU.mult, op1=ALU.add))
            b.wait("tensor", DV, dv_c1)
            pe_c1b = mm(PS[6][:, 0:512], onesk[:], rt1[0:1, 0:512], True, True, inc=True)
            b.wait("vector", PE, pe_c1b)
            dv_bc1 = vop(lambda e: e.tensor_copy(out=bcA[:], in_=PS[6][:, 0:512]))
            ldu = cast(wvub[:].rearrange("p (a n) -> p a n", a=2),
                       WVUT.rearrange("(a p) n -> p a n", p=128))
            b.wait("tensor", GP, ldu)
            b.wait("tensor", DV, dv_a2)
            b.wait("tensor", DV, dv_bc1)
            ep_dv = {6: 0, 7: 0}
            for f in range(NCH):
                fb = 6 + (f % 2)
                b.wait("tensor", DV, ep_dv[fb])
                for v2 in range(2):
                    mm(PS[fb][:, 0:512], wvub[:, 4096 * v2 + 128 * f:4096 * v2 + 128 * f + 128],
                       a2sb[:, 512 * v2:512 * v2 + 512], v2 == 0, False)
                pe_f = mm(PS[fb][:, 0:512], oneskb[:], c2bf[0:1, 0:512], False, True, inc=True)
                b.wait("vector", PE, pe_f)
                vop(lambda e, f=f: e.tensor_tensor(out=gx[:], in0=xh[:, 512 * f:512 * f + 512],
                                                   in1=bcA[:], op=ALU.mult), inc=False)
                ep_dv[fb] = vop(lambda e, f=f, fb=fb: e.tensor_tensor(
                    out=xh[:, 512 * f:512 * f + 512], in0=gx[:], in1=PS[fb][:, 0:512],
                    op=ALU.add))
            dv_ep = b.snap(DV)
            dv_ep_prev = dv_ep
            b.wait("sync", DV, dv_ep)
            st = dma("sync", SY, dst[:, sl].rearrange("(c p) t -> p c t", p=128),
                     xh[:].rearrange("p (c t) -> p c t", t=512))
        state["dv_tail"] = dv_ep_prev
        state["prev_store"] = st
        return st


    def attention(src, dst):
        b.wait("tensor", DV, state["dv_tail"])
        AB = WBASE
        K_sb = b.at("K_sb", [128, S], BF16, AB)
        V_sb = b.at("V_sb", [128, S], BF16, AB + 16384)
        xq = [b.at("xq0", [128, 32 * 512], BF16, AB + 32768),
              b.at("xq1", [128, 32 * 512], BF16, AB + 65536)]
        wqkv = b.at("wqkv", [128, 32 * 768], BF16, AB + 98304)
        rpc = [b.at("rpc0", [128, 512], BF16, AB + 147456),
               b.at("rpc1", [128, 512], BF16, AB + 148480)]
        rps = [b.at("rps0", [128, 512], BF16, AB + 149504),
               b.at("rps1", [128, 512], BF16, AB + 150528)]
        racc = b.at("racc", [128, 512], F32, AB + 151552)
        rtmp = b.at("rtmp", [128, 512], F32, AB + 153600)
        qkf = b.at("qkf", [128, 6 * 512], BF16, AB + 158720)
        qst = [b.at("qst0", [128, 512], BF16, AB + 155648),
               b.at("qst1", [128, 512], BF16, AB + 156672)]
        vtmp = b.at("vtmp", [128, 512], BF16, AB + 157696)
        ML = AB + 32768
        q_tile = [b.at(f"q_tile{i}", [128, 4 * 512], BF16, ML + 4096 * i) for i in range(2)]
        probp = [b.at(f"probp{i}", [128, 1024], BF16, ML + 8192 + 2048 * i) for i in range(2)]
        attn_t = b.at("attn_t", [128, 4 * 512], BF16, ML + 16384)
        s_sb = b.at("s_sb", [4, 512], F32, ML + 20480)
        rcp = b.at("rcp", [4, 512], F32, ML + 22528)
        bcs = [b.at("bcs0", [128, 512], F32, ML + 24576),
               b.at("bcs1", [128, 512], F32, ML + 26624)]
        xh_w = b.at("xh_w", [128, 32 * 512], F32, AB)
        at_half = b.at("at_half", [128, 32 * 512], BF16, AB + 65536)
        wob = [b.at(f"wob{i}", [128, 4 * 512], BF16, AB + 98304 + 4096 * i) for i in range(4)]

        sq_pe = {0: 0, 1: 0}
        ac_sq = 0
        stat_pe = 0
        xst = 0
        dv_v_prev = 0
        b.wait("sync", SY, state["prev_store"])
        for h in (0, 1):
            if h == 1:
                b.wait("sync", PE, stat_pe)
                b.wait("sync", AC, ac_sq)
                b.wait("sync", DV, dv_hd)
            lds = load_xhalf(src, slice(512 * h, 512 * h + 512), xh)
            b.wait("tensor", DV, dv_v_prev if h == 1 else state["dv_tail"])
            for c in range(NCH):
                if c % 8 == 0:
                    b.wait("scalar", SY, lds[c // 8])
                    b.wait("tensor", SY, lds[c // 8])
                if c >= 2 or h == 1:
                    b.wait("scalar", PE, sq_pe[c % 2])
                ac_sq = aop(lambda e, c=c: e.activation(
                    out=sqb[c % 2][:], in_=xh[:, 512 * c:512 * c + 512], func=AF.Square))
                b.wait("tensor", AC, ac_sq)
                mm(PS[0][0:1, 0:512], ones128[:], xh[:, 512 * c:512 * c + 512],
                   c == 0, c == NCH - 1)
                sq_pe[c % 2] = mm(PS[2][0:1, 0:512], ones128[:], sqb[c % 2][:],
                                  c == 0, c == NCH - 1, inc=True)
            stat_pe = b.snap(PE)
            sl = slice(512 * h, 512 * h + 512)
            b.wait("vector", PE, stat_pe)
            vop(lambda e, sl=sl: e.tensor_scalar_mul(m_row[0:1, sl], PS[0][0:1, 0:512],
                                                     1.0 / D), inc=False)
            dv_m = vop(lambda e: e.engine_nop())
            b.wait("vector", DV, dv_m)
            vop(lambda e, sl=sl: e.tensor_tensor(out=rt1[0:1, 0:512], in0=m_row[0:1, sl],
                                                 in1=m_row[0:1, sl], op=ALU.mult), inc=False)
            dv_v = vop(lambda e: e.scalar_tensor_tensor(
                out=rt2[0:1, 0:512], in0=PS[2][0:1, 0:512], scalar=1.0 / D,
                in1=rt1[0:1, 0:512], op0=ALU.mult, op1=ALU.subtract))
            dv_v_prev = dv_v
            b.wait("scalar", DV, dv_v)
            ac_ln = aop(lambda e: e.activation(out=rt3[0:1, 0:512], in_=rt2[0:1, 0:512],
                                               func=AF.Ln, bias=csts[0:1, 0:1]))
            b.wait("scalar", AC, ac_ln)
            ac_rstd = aop(lambda e, sl=sl: e.activation(out=rstd_row[0:1, sl],
                                                        in_=rt3[0:1, 0:512],
                                                        func=AF.Exp, scale=-0.5))
            b.wait("tensor", AC, ac_rstd)
            b.wait("tensor", DV, dv_v)
            mm(PS[5][:, 0:512], onesk[:], m_row[0:1, sl], True, True)
            pe_bc = mm(PS[6][:, 0:512], onesk[:], rstd_row[0:1, sl], True, True, inc=True)
            b.wait("vector", PE, pe_bc)
            vop(lambda e: e.tensor_copy(out=bcA[:], in_=PS[5][:, 0:512]), inc=False)
            vop(lambda e: e.tensor_copy(out=bcB[:], in_=PS[6][:, 0:512]), inc=False)
            for c in range(NCH):
                vop(lambda e, c=c: e.tensor_tensor(out=gx[:], in0=xh[:, 512 * c:512 * c + 512],
                                                   in1=bcA[:], op=ALU.subtract), inc=False)
                vop(lambda e, c=c: e.tensor_tensor(out=h_sb[:, 512 * c:512 * c + 512],
                                                   in0=gx[:], in1=bcB[:], op=ALU.mult),
                    inc=(c == NCH - 1))
            dv_hd = b.snap(DV)
            b.wait("sync", DV, dv_hd)
            xst = dma("sync", SY, xh_in[h].rearrange("(c p) t -> p c t", p=128),
                      h_sb[:].rearrange("p (c t) -> p c t", t=512))
            b.wait("gpsimd", SY, xst)
            cc2 = b.emit("gpsimd", lambda e, h=h: e.collective_compute(
                "AllGather", ALU.bypass, ins=[xh_in[h].opt()],
                outs=[xh_all[h].opt()],
                replica_groups=[list(range(NCORES))]), inc=CC, n=1)

        wq_ld = cast(wqkv[:].rearrange("p (c f) -> p c f", f=768)[:, :, 0:512],
                     WQ.rearrange("(c p) f -> p c f", p=128))
        wk_ld = cast(wqkv[:].rearrange("p (c f) -> p c f", f=768)[:, :, 512:640],
                     WK.rearrange("(c p) f -> p c f", p=128))
        wv_ld = cast(wqkv[:].rearrange("p (c f) -> p c f", f=768)[:, :, 640:768],
                     WV.rearrange("(c p) f -> p c f", p=128))
        b.wait("tensor", GP, wv_ld)
        b.wait("sync", CC, cc2)
        xq_pe = {0: 0, 1: 0}
        rp_dv = {0: 0, 1: 0}
        qst_sy = {0: 0, 1: 0}
        dv_q_tail = 0
        for tb in range(16):
            tm = tb % 2
            rank, lt = tb // 2, (tb % 2) * 512
            b.wait("sync", PE, xq_pe[tm])
            lx = dma("sync", SY, xq[tm][:].rearrange("p (c t) -> p c t", t=512),
                     xh_all[tb % 2][D * rank:D * rank + D, :]
                     .rearrange("(c p) t -> p c t", p=128))
            b.wait("sync", DV, rp_dv[tm])
            lrc = dma("sync", SY, rpc[tm][:], ROPEC[:, 512 * tb:512 * tb + 512])
            lrs = dma("sync", SY, rps[tm][:], ROPES[:, 512 * tb:512 * tb + 512])
            b.wait("tensor", SY, lrs)
            if isinstance(dv_q_tail, tuple):
                b.wait("tensor", DV, dv_q_tail[0])
                b.wait("tensor", AC, dv_q_tail[1])
            for c in range(NCH):
                for f in range(6):
                    xq_pe[tm] = mm(PS[f][:, 0:512],
                                   wqkv[:, 768 * c + 128 * f:768 * c + 128 * f + 128],
                                   xq[tm][:, 512 * c:512 * c + 512],
                                   c == 0, c == NCH - 1, inc=(f == 5))
            pe_qkv = b.snap(PE)
            b.wait("vector", PE, pe_qkv)
            b.wait("scalar", PE, pe_qkv)
            for fi in range(3):
                vop(lambda e, fi=fi: e.tensor_copy(out=qkf[:, 512 * fi:512 * fi + 512],
                                                   in_=PS[fi][:, 0:512]),
                    inc=(fi == 2))
            for fi in range(3, 6):
                aop(lambda e, fi=fi: e.activation(out=qkf[:, 512 * fi:512 * fi + 512],
                                                  in_=PS[fi][:, 0:512], func=AF.Identity),
                    inc=(fi == 5))
            dv_ev = b.snap(DV)
            ac_ev = b.snap(AC)
            dv_q_tail = (dv_ev, ac_ev)
            b.wait("vector", AC, ac_ev)
            for fi in range(5):
                dst_is_k = (fi == 4)
                qm = fi % 2
                if not dst_is_k:
                    b.wait("vector", SY, qst_sy[qm])
                vop(lambda e, fi=fi, tm=tm: e.tensor_tensor(
                    out=racc[:], in0=qkf[:, 512 * fi:512 * fi + 512], in1=rpc[tm][:],
                    op=ALU.mult), inc=False)
                vop(lambda e, fi=fi, tm=tm: e.tensor_tensor(
                    out=rtmp[0:64, :], in0=qkf[64:128, 512 * fi:512 * fi + 512],
                    in1=rps[tm][64:128, :], op=ALU.mult), inc=False)
                vop(lambda e, fi=fi, tm=tm: e.tensor_tensor(
                    out=rtmp[64:128, :], in0=qkf[0:64, 512 * fi:512 * fi + 512],
                    in1=rps[tm][0:64, :], op=ALU.mult), inc=False)
                if dst_is_k:
                    dv_ro = vop(lambda e, tb=tb: e.tensor_tensor(
                        out=K_sb[:, 512 * tb:512 * tb + 512], in0=racc[:], in1=rtmp[:],
                        op=ALU.add))
                else:
                    dv_ro = vop(lambda e, qm=qm: e.tensor_tensor(
                        out=qst[qm][:], in0=racc[:], in1=rtmp[:], op=ALU.add))
                    b.wait("sync", DV, dv_ro)
                    qst_sy[qm] = dma("sync", SY, q_scr[128 * fi:128 * fi + 128,
                                                      512 * tb:512 * tb + 512], qst[qm][:])
            dv_vt = vop(lambda e: e.tensor_copy(out=vtmp[:], in_=qkf[:, 2560:3072]))
            b.wait("tensor", DV, dv_vt)
            pst7 = PS[7][:, 0:512].bitcast(BF16)
            dv_tr = 0
            for ts4 in range(4):
                if ts4 > 0:
                    b.wait("tensor", DV, dv_tr)
                pe_tr = b.emit("tensor", lambda e, ts4=ts4, pst7=pst7: e.transpose(
                    pst7[:, 0:128], vtmp[:, 128 * ts4:128 * ts4 + 128], ident[:]), inc=PE)
                b.wait("vector", PE, pe_tr)
                dv_tr = vop(lambda e, tb=tb, ts4=ts4, pst7=pst7: e.tensor_copy(
                    out=V_sb[:, 512 * tb + 128 * ts4:512 * tb + 128 * ts4 + 128],
                    in_=pst7[:, 0:128]))
            rp_dv[tm] = dv_tr
        dv_qkv_done = dv_tr
        qspill_all = b.snap(SY)

        b.wait("tensor", DV, dv_qkv_done)
        STP = PSALL[:, 2048:3072]
        ONESB = PSALL[0:4, 3072:3584]
        BCB = PSALL[:, 3072:3584]
        ac_prev = 0
        probs_pe = {0: 0, 1: 0}
        dv_at = 0
        at_sy = 0
        pe_tail = 0
        b.wait("sync", SY, qspill_all)
        lq = {}
        lq[0] = dma("sync", SY, q_tile[0][:].rearrange("p (g t) -> p g t", t=512),
                    q_scr[:, 0:512].rearrange("(g p) t -> p g t", p=128))
        for t in range(16):
            if t < 15:
                b.wait("sync", PE, pe_tail)
                lq[(t + 1) % 2] = dma(
                    "sync", SY, q_tile[(t + 1) % 2][:].rearrange("p (g t) -> p g t", t=512),
                    q_scr[:, 512 * (t + 1):512 * (t + 1) + 512]
                    .rearrange("(g p) t -> p g t", p=128))
            qt = q_tile[t % 2]
            b.wait("tensor", SY, lq[t % 2])
            nkb = 4 * (t + 1)
            npair = nkb * 2
            pend = None

            def emit_onespv(pinfo, nkb=nkb):
                pi, infos, par, acv = pinfo
                if pi == 0:
                    b.wait("tensor", DV, dv_at)
                b.wait("tensor", AC, acv)
                for (si, kb, g) in infos:
                    px = probp[par][:, 512 * si:512 * si + 512]
                    mm(PS[g], V_sb[:, 128 * kb:128 * kb + 128], px,
                       kb == 0, kb == nkb - 1)
                for ii, (si, kb, g) in enumerate(infos):
                    px = probp[par][:, 512 * si:512 * si + 512]
                    probs_pe[par] = mm(ONESB, eg[:, 4 * g:4 * g + 4], px,
                                       kb == 0 and g == 0, kb == nkb - 1 and g == 3,
                                       inc=(ii == 1))

            for p in range(npair):
                par = p % 2
                b.wait("tensor", AC, ac_prev)
                for si in range(2):
                    stp = 2 * p + si
                    kb, g = stp // 4, stp % 4
                    j = kb - 4 * t
                    mm(STP[:, 512 * si:512 * si + 512], K_sb[:, 128 * kb:128 * kb + 128],
                       qt[:, 512 * g:512 * g + 512], True, j < 0,
                       inc=(si == 1 and j < 0))
                    if j >= 0:
                        mm(STP[:, 512 * si:512 * si + 512], ident[:],
                           masks[:, 512 * j:512 * j + 512], False, True,
                           inc=(si == 1))
                pe_qk = b.snap(PE)
                b.wait("scalar", PE, pe_qk)
                b.wait("scalar", PE, probs_pe[par])
                ac_prev = aop(lambda e, par=par: e.activation(
                    out=probp[par][:], in_=STP, func=AF.Exp,
                    bias=csts[:, 2:3], scale=SC_ATT))
                infos = [(si, (2 * p + si) // 4, (2 * p + si) % 4) for si in range(2)]
                if pend is not None:
                    emit_onespv(pend)
                pend = (p, infos, par, ac_prev)
            emit_onespv(pend)
            pe_tail = b.snap(PE)
            b.wait("vector", PE, pe_tail)
            b.wait("vector", SY, at_sy)
            dv_s = vop(lambda e: e.tensor_copy(out=s_sb[:], in_=ONESB))
            b.wait("vector", DV, dv_s)
            dv_rcp = vop(lambda e: e.reciprocal(rcp[:], s_sb[:]))
            b.wait("tensor", DV, dv_rcp)
            dv_bcg = 0
            for g in range(4):
                if g > 0:
                    b.wait("tensor", DV, dv_bcg)
                pe_bc = mm(BCB, sel4[:, 128 * g:128 * g + 128], rcp[:],
                           True, True, inc=True)
                b.wait("vector", PE, pe_bc)
                dv_bcg = vop(lambda e, g=g: e.tensor_copy(out=bcs[g % 2][:], in_=BCB))
                b.wait("vector", DV, dv_bcg)
                vop(lambda e, g=g: e.tensor_tensor(
                    out=attn_t[:, 512 * g:512 * g + 512], in0=PS[g],
                    in1=bcs[g % 2][:], op=ALU.mult), inc=(g == 3))
            dv_at = b.snap(DV)
            b.wait("sync", DV, dv_at)
            for g in range(4):
                at_sy = dma("sync", SY,
                            at_in[t // 8][128 * g:128 * g + 128,
                                          512 * (t % 8):512 * (t % 8) + 512],
                            attn_t[:, 512 * g:512 * g + 512])
            if t == 7:
                b.wait("gpsimd", SY, at_sy)
                b.emit("gpsimd", lambda e: e.collective_compute(
                    "AllGather", ALU.bypass, ins=[at_in[0].opt()], outs=[at_all[0].opt()],
                    replica_groups=[list(range(NCORES))]), inc=CC, n=1)
        b.wait("gpsimd", SY, at_sy)
        cc3 = b.emit("gpsimd", lambda e: e.collective_compute(
            "AllGather", ALU.bypass, ins=[at_in[1].opt()], outs=[at_all[1].opt()],
            replica_groups=[list(range(NCORES))]), inc=CC, n=1)
        at_view = [a.rearrange("e (s t) -> (e s) t", t=512) for a in at_all]
        bndreg = {}

        def _mkbnd(e):
            bndreg["r"] = e.alloc_register("gbnd")
            e.reg_mov(bndreg["r"], D * 8 - 1)
        b.emit("gpsimd", _mkbnd)
        wo_pe = {0: 0, 1: 0, 2: 0, 3: 0}
        st = 0
        dv_wo_prev = dv_at
        pe_gth_gate = 0
        for h in (0, 1):
            sl = slice(512 * h, 512 * h + 512)
            b.wait("tensor", DV, dv_wo_prev)
            b.wait("sync", SY, max(st, state["prev_store"]))
            lxw = dma("sync", SY, xh_w[:].rearrange("p (c t) -> p c t", t=512),
                      src[:, sl].rearrange("(c p) t -> p c t", p=128))
            b.wait("gpsimd", CC, cc3)
            b.wait("gpsimd", PE, pe_gth_gate)
            gth = 0
            for ce in range(NCH):
                for vh in range(2):
                    gth = b.emit("gpsimd", lambda e, ce=ce, h=h, vh=vh: e.indirect_dma_start(
                        out=at_half[:, 512 * ce:512 * ce + 512], out_offset=None,
                        in_=at_view[vh],
                        in_offset=IndirectOffsetOnAxis(
                            ap=idxwo2[vh][:, 2 * ce + h:2 * ce + h + 1], axis=0),
                        bounds_check=bndreg["r"], oob_is_err=False), inc=GP, n=16)
            b.wait("tensor", GP, gth)
            b.wait("vector", SY, lxw)
            ep_prev = {0: 0, 1: 0, 2: 0, 3: 0}
            for fg in range(8):
                for f4 in range(4):
                    b.wait("tensor", DV, ep_prev[f4])
                for em in range(8):
                    emm = em % 4
                    b.wait("gpsimd", PE, wo_pe[emm])
                    lw = cast(wob[emm][:].rearrange("p (a f) -> p a f", a=4),
                              WO[512 * em:512 * em + 512, 512 * fg:512 * fg + 512]
                              .rearrange("(a p) f -> p a f", p=128))
                    b.wait("tensor", GP, lw)
                    for es in range(4):
                        for f4 in range(4):
                            wo_pe[emm] = mm(
                                PS[f4][:, 0:512],
                                wob[emm][:, 512 * es + 128 * f4:512 * es + 128 * f4 + 128],
                                at_half[:, 512 * (4 * em + es):512 * (4 * em + es) + 512],
                                em == 0 and es == 0, em == 7 and es == 3,
                                inc=(es == 3 and f4 == 3))
                pe_fg = b.snap(PE)
                b.wait("vector", PE, pe_fg)
                for f4 in range(4):
                    f = 4 * fg + f4
                    ep_prev[f4] = vop(lambda e, f=f, f4=f4: e.tensor_tensor(
                        out=xh_w[:, 512 * f:512 * f + 512], in0=xh_w[:, 512 * f:512 * f + 512],
                        in1=PS[f4][:, 0:512], op=ALU.add))
            pe_gth_gate = b.snap(PE)
            dv_wo_prev = b.snap(DV)
            b.wait("sync", DV, dv_wo_prev)
            st = dma("sync", SY, dst[:, sl].rearrange("(c p) t -> p c t", p=128),
                     xh_w[:].rearrange("p (c t) -> p c t", t=512))
        state["dv_tail"] = dv_wo_prev
        state["prev_store"] = st
        return st

    st1 = ffn(X, x_dram if stage >= 2 else Y, s_in[0], s_out[0])
    state["prev_store"] = st1
    if stage >= 2:
        st2 = attention(x_dram, x_dram if stage >= 3 else Y)
        if stage >= 3:
            ffn(x_dram, Y, s_in[1], s_out[1])

    b.wait("sync", SY, b.snap(SY))
    b.wait("gpsimd", GP, b.snap(GP))
    b.wait("vector", DV, b.snap(DV))
    b.wait("scalar", AC, b.snap(AC))
    b.wait("tensor", PE, b.snap(PE))
    b.check()

    with nc.Block() as block:
        b.run_streams(block)
    return nc



def _host_inputs(inputs):
    x = np.asarray(inputs["L0_attn_x"])[0]
    w_in = np.asarray(inputs["L0_attn_w_in"])
    w_out = np.asarray(inputs["L0_attn_w_out"])
    Wq = np.asarray(inputs["L0_ffn_Wq"]); Wr = np.asarray(inputs["L0_ffn_Wr"])
    ak = np.asarray(inputs["L0_ffn_aux_keys"]); av = np.asarray(inputs["L0_ffn_aux_values"])
    Wvd = np.asarray(inputs["L0_ffn_Wvd"]); Wvu = np.asarray(inputs["L0_ffn_Wvu"])
    Wu1 = np.asarray(inputs["L0_ffn_Wu1"]); bu1 = np.asarray(inputs["L0_ffn_bu1"])
    Wu2 = np.asarray(inputs["L0_ffn_Wu2"]); bu2 = np.asarray(inputs["L0_ffn_bu2"])
    gw1 = np.asarray(inputs["L0_ffn_gate_w1"]); gb = np.asarray(inputs["L0_ffn_gate_bias"])
    logrel = np.asarray(inputs["L0_ffn_log_rel"])

    C = np.ascontiguousarray
    inv_freq = 1.0 / (10000.0 ** (np.arange(0, 128, 2, dtype=np.float64) / 128.0))
    ang = np.arange(S, dtype=np.float64)[:, None] * inv_freq[None, :]
    cosv = np.cos(ang).astype(np.float32); sinv = np.sin(ang).astype(np.float32)
    rope_c = np.concatenate([cosv.T, cosv.T], axis=0).astype(BF16NP)
    rope_s = np.concatenate([sinv.T, -sinv.T], axis=0).astype(BF16NP)
    masksm = np.zeros((128, 2048), dtype=np.float32)
    kk = np.arange(128)[:, None]; qq = np.arange(512)[None, :]
    for j in range(4):
        masksm[:, 512 * j:512 * j + 512] = np.where(128 * j + kk <= qq, 0.0, -1e30)
    identt = np.eye(128, dtype=np.float32).astype(BF16NP)
    egm = np.zeros((128, 16), dtype=BF16NP)
    for g in range(4):
        egm[:, 4 * g + g] = 1.0
    sel4m = np.zeros((4, 512), dtype=np.float32)
    for g in range(4):
        sel4m[g, 128 * g:128 * g + 128] = 1.0
    cstsm = np.zeros((128, 8), dtype=np.float32)
    cstsm[:, 0] = 1e-5; cstsm[:, 1] = 1.0; cstsm[:, 2] = -BCONST
    wu2c = C(Wu2.reshape(8, 128).T.astype(np.float32))
    bu1cm = C(bu1.reshape(8, 128).T.astype(np.float32))
    sc3m = np.zeros((1, 8), dtype=np.float32)
    sc3m[0, 0] = float(np.reshape(bu2, -1)[0]); sc3m[0, 1] = float(gw1); sc3m[0, 2] = float(gb)

    common = dict(
        w_o=C(w_out.T), wq_t=C(Wq.T), wr_t=C(Wr.T), ak_t=C(ak.T), av=C(av),
        wvd_t=C(Wvd.T), wvu_t=C(Wvu.T), wu1_t=C(Wu1.T),
        wu2_c=wu2c, bu1_c=bu1cm, sc3=sc3m,
        rope_c=rope_c, rope_s=rope_s, masks=masksm.astype(BF16NP), identt=identt,
        eg=egm, sel4=sel4m, onesk=np.ones((1, 128), np.float32),
        oneskb=np.ones((1, 128), BF16NP),
        e8kb=np.full((1, 128), 8.0).astype(BF16NP),
        logrelb=C(logrel.reshape(1, 1024)).astype(BF16NP),
        ones128=np.ones((128, 1), np.float32), csts=cstsm,
    )

    def qrow(j):
        return 1024 * (j // 8) + 128 * (j % 8)

    in_maps = []
    for c in range(NCORES):
        d = dict(common)
        d["x0"] = C(x[TPC * c:TPC * (c + 1)].T)
        wq_rows = np.concatenate([w_in[qrow(4 * c + g):qrow(4 * c + g) + 128]
                                  for g in range(4)], axis=0)
        d["w_q"] = C(wq_rows.T)
        d["w_k"] = C(w_in[4096 + 128 * c:4096 + 128 * c + 128].T)
        d["w_v"] = C(w_in[5120 + 128 * c:5120 + 128 * c + 128].T)
        OOB = np.uint32(1 << 26)
        idx0 = np.full((128, 64), OOB, dtype=np.uint32)
        idx1 = np.full((128, 64), OOB, dtype=np.uint32)
        for ce in range(NCH):
            for hh in range(2):
                e = 128 * ce + np.arange(128)
                sgl = 2 * c + hh
                if sgl < 8:
                    idx0[:, 2 * ce + hh] = (e * 8 + sgl).astype(np.uint32)
                else:
                    idx1[:, 2 * ce + hh] = (e * 8 + (sgl - 8)).astype(np.uint32)
        d["idx_wo"] = idx0
        d["idx_wo1"] = idx1
        in_maps.append(d)
    return in_maps


def kernel(**inputs):
    in_maps = _host_inputs(inputs)
    nc = build_program(_STAGE)
    res = run_bass_kernel_spmd(nc, in_maps, core_ids=list(range(NCORES)))
    out = np.empty((1, S, D), dtype=np.float32)
    for c in range(NCORES):
        out[0, TPC * c:TPC * (c + 1), :] = res.results[c]["y"].T
    return out
